# version 08
# baseline (speedup 1.0000x reference)
import numpy as np
import concourse.bass as bass
import concourse.mybir as mybir
from concourse.bass_utils import run_bass_kernel_spmd
from contextlib import ExitStack

F32 = mybir.dt.float32
BF16 = mybir.dt.bfloat16
I32 = mybir.dt.int32
AF = mybir.ActivationFunctionType
ALU = mybir.AluOpType

N_CORES = 8
N = 4096
C = 1024
H = 16
E = 64
BS = 64
I3 = 3072
ST = 512
NS = N // ST
CH = C // 128
OCH = I3 // 128
LN_EPS = 1e-5
ATT_EPS = 1e-3
QMAX = 32767.0

PE, ACT, DVE, GP, SY = "pe", "act", "dve", "gp", "sy"
ENG_SEM = {PE: "pe", ACT: "act", DVE: "dve", GP: "gp"}


class Sched:

    def __init__(self, sems):
        self.sems = sems
        self.cnt = {k: 0 for k in sems}
        self.last_wait = {}
        self.cur = None
        self.eng = None

    def begin(self, name, eng):
        self.cur = name
        self.eng = eng
        self.cnt = {k: 0 for k in self.sems}
        self.last_wait = {}

    def do(self, eng_name, fn, sem=None, inc=None):
        if sem is None:
            sem = ENG_SEM[eng_name]
        if inc is None:
            inc = 1 if sem in ("pe", "act", "dve", "gp", "cc") else 16
        if eng_name in (ACT, DVE, GP) and sem == ENG_SEM[eng_name]:
            v = self.cnt[sem]
            key = (eng_name, sem)
            if v > 0 and self.last_wait.get(key, -1) < v:
                self.last_wait[key] = v
                if eng_name == self.cur:
                    self.eng.wait_ge(self.sems[sem], v)
        if eng_name == self.cur:
            inst = fn(self.eng)
            inst.then_inc(self.sems[sem], inc)
        self.cnt[sem] += inc

    def mark(self, sem):
        return self.cnt[sem]

    def wait(self, eng_name, sem, val=None):
        v = self.cnt[sem] if val is None else val
        assert v <= self.cnt[sem], f"wait on future: {sem} {v} > {self.cnt[sem]}"
        if v <= 0:
            return
        if sem == ENG_SEM.get(eng_name):
            return
        key = (eng_name, sem)
        if self.last_wait.get(key, -1) >= v:
            return
        self.last_wait[key] = v
        if eng_name == self.cur:
            self.eng.wait_ge(self.sems[sem], v)


def build_nc():
    nc = bass.Bass("TRN2", target_bir_lowering=False)

    x_t = nc.dram_tensor("x_t", [C, N], F32, kind="ExternalInput")
    qkvw_t = nc.dram_tensor("qkvw_t", [C, 3 * C], F32, kind="ExternalInput")
    projw_t = nc.dram_tensor("projw_t", [C, C], F32, kind="ExternalInput")
    projb_t = nc.dram_tensor("projb_t", [128, CH], F32, kind="ExternalInput")
    gatew = nc.dram_tensor("gatew", [I3, C], F32, kind="ExternalInput")
    upw = nc.dram_tensor("upw", [I3, C], F32, kind="ExternalInput")
    downw = nc.dram_tensor("downw", [C, I3], F32, kind="ExternalInput")
    out_t = nc.dram_tensor("out_t", [C, N], F32, kind="ExternalOutput")

    r1_d = nc.dram_tensor("r1_d", [C, N], F32)
    t_d = nc.dram_tensor("t_d", [C, N], BF16)
    graw_d = nc.dram_tensor("graw_d", [I3, N], BF16)
    uraw_d = nc.dram_tensor("uraw_d", [I3, N], BF16)
    g2_d = nc.dram_tensor("g2_d", [I3, N], BF16)
    o_d = nc.dram_tensor("o_d", [I3, N], BF16)
    gdq_d = nc.dram_tensor("gdq_d", [I3, C], BF16)
    udq_d = nc.dram_tensor("udq_d", [I3, C], BF16)
    ddq_d = nc.dram_tensor("ddq_d", [C, I3], BF16)
    mx_d = nc.dram_tensor("mx_d", [1, 128], F32)
    ones_d = nc.dram_tensor("ones_d", [1, 64], BF16)
    cc_in = nc.dram_tensor("cc_in", [1, 2], F32)
    cc_out = nc.dram_tensor("cc_out", [1, 2], F32)

    ctx = ExitStack()
    with ctx:
        def sb(name, shape, dt=F32):
            return ctx.enter_context(nc.sbuf_tensor(name, shape, dt))

        def ps(name, shape):
            return ctx.enter_context(nc.psum_tensor(name, shape, F32))

        sems = {}
        for s_ in ("pe", "act", "dve", "gp", "cc",
                   "lsel", "lw1", "lw2", "lsc", "lx", "lt", "lu", "lo",
                   "lwT", "ld2", "sw0", "sw1", "sr1", "stt", "sf0", "sf1",
                   "sg2", "so", "sout", "lt2", "lu2", "sg3", "so2"):
            sems[s_] = ctx.enter_context(nc.semaphore("s_" + s_))

        W1 = sb("W1", [128, 24576], BF16)
        W2a = sb("W2a", [128, 8192], BF16)
        W2b = sb("W2b", [128, 16384], BF16)
        BF1 = sb("BF1", [128, 4096], F32)
        OST = sb("OST", [128, 2048], F32)
        BIGI = sb("BIGI", [128, 1024], I32)
        BB1 = sb("BB1", [128, 4096], BF16)
        BB2 = sb("BB2", [128, 4096], BF16)
        BB3 = sb("BB3", [128, 4096], BF16)
        BB4 = sb("BB4", [128, 4096], BF16)
        ARB = sb("ARB", [128, 4096], BF16)
        fst = sb("fst", [128, 2048], BF16)

        stat = sb("stat", [128, 1024])
        Sf = sb("Sf", [128, 1024])
        Sb = sb("Sb", [128, 1024], BF16)
        zf = sb("zf", [128, 8])
        zbd = sb("zbd", [128, 16], BF16)
        d2 = sb("d2", [2, 4096], BF16)
        sel2 = sb("sel2", [2, 128], BF16)
        serb = sb("serb", [16, 512], BF16)
        rbcb = sb("rbcb", [128, 512], BF16)
        sel = sb("sel", [128, 128], BF16)
        selT = sb("selT", [16, 1024], BF16)
        ones = sb("ones", [128, 128], BF16)
        onecol = sb("onecol", [128, 1], BF16)
        mxacc = sb("mxacc", [128, 8])
        rowred = sb("rowred", [1, 128])
        scl = sb("scl", [128, 32])
        bm = sb("bm", [128, 32])
        bi = sb("bi", [128, 32])
        projb = sb("projb", [128, CH])

        p_mm = ps("p_mm", [128, 512])
        p_sum = ps("p_sum", [128, 512])
        p_sq = ps("p_sq", [128, 512])
        p_bc = ps("p_bc", [128, 512])
        p_att = ps("p_att", [128, 1024])
        p_S = ps("p_S", [128, 1024])

        block = ctx.enter_context(nc.Block())
        S = Sched(sems)

        sT, iT = 1, 2
        sG, rG, nbG, sG65, nbG65 = 5, 6, 7, 8, 9
        sU, rU = 10, 11
        s2, r2, s2su, nb2su = 13, 14, 15, 16
        sO, rO = 18, 19

        def uT_sl(k, oc):
            f = 3072 * k + 128 * oc
            return W2a[:, f:f + 128] if f < 8192 else W2b[:, f - 8192:f - 8192 + 128]

        def oq_sl(k):
            f = 512 * k
            return W2a[:, f:f + 512] if f + 512 <= 8192 else W2b[:, f - 8192:f - 8192 + 512]

        def program():
            S.do(DVE, lambda e: e.memset(ones[:], 1.0))
            S.do(DVE, lambda e: e.memset(onecol[:], 1.0))
            S.do(DVE, lambda e: e.memset(sel[:], 0.0))
            S.do(DVE, lambda e: e.memset(selT[:], 0.0))
            for c in range(CH):
                for j in range(2):
                    h = 2 * c + j
                    S.do(DVE, lambda e, c=c, j=j, h=h: e.memset(
                        sel[64 * j:64 * j + 64, 16 * c + h:16 * c + h + 1], 1.0))
            S.do(DVE, lambda e: e.memset(sel2[:], 0.0))
            S.do(DVE, lambda e: e.memset(Sf[:], 0.0))
            S.do(DVE, lambda e: e.memset(Sb[:], 0.0))
            S.do(DVE, lambda e: e.memset(zf[:], 0.0))
            S.do(DVE, lambda e: e.memset(zbd[:], 0.0))
            S.do(DVE, lambda e: e.memset(mxacc[:], 0.0))
            for col, v in ((24, LN_EPS), (25, -0.5), (26, 8.0), (27, -1.0),
                           (28, 32768.0), (29, -32768.0)):
                S.do(DVE, lambda e, col=col, v=v: e.memset(
                    scl[:, col:col + 1], v))
            consts_dve = S.mark("dve")

            S.wait(SY, "dve", consts_dve)
            S.do(SY, lambda e: e.dma_start(out=ones_d[:, :], in_=ones[0:1, 0:64]),
                 sem="lsel")
            S.wait(SY, "lsel")
            for c in range(CH):
                for j in range(2):
                    h = 2 * c + j
                    S.do(SY, lambda e, h=h, c=c, j=j: e.dma_start(
                        out=selT[h:h + 1, 128 * c + 64 * j:128 * c + 64 * j + 64],
                        in_=ones_d[:, :]), sem="lsel")
            for j in range(2):
                S.do(SY, lambda e, j=j: e.dma_start(
                    out=sel2[j:j + 1, 64 * j:64 * j + 64], in_=ones_d[:, :]),
                    sem="lsel")
            sel_ld = S.mark("lsel")

            for c in range(CH):
                S.do(SY, lambda e, c=c: e.dma_start(
                    out=BF1[:, 0:3072], in_=qkvw_t[128 * c:128 * (c + 1), :]),
                    sem="lw1")
                S.wait(ACT, "lw1")
                S.do(ACT, lambda e, c=c: e.activation(
                    W1[:, 3072 * c:3072 * (c + 1)], BF1[:, 0:3072], AF.Copy,
                    bias=0.0, scale=1.0))
                S.do(SY, lambda e, c=c: e.dma_start(
                    out=OST[:, 0:1024], in_=projw_t[128 * c:128 * (c + 1), :]),
                    sem="lw2")
                S.wait(ACT, "lw2")
                S.do(ACT, lambda e, c=c: e.activation(
                    W2a[:, 1024 * c:1024 * (c + 1)], OST[:, 0:1024], AF.Copy,
                    bias=0.0, scale=1.0))
                S.wait(SY, "act")
            S.do(SY, lambda e: e.dma_start(out=projb[:, :], in_=projb_t[:, :]),
                 sem="lsc")
            projb_ld = S.mark("lsc")

            S.wait(SY, "dve", consts_dve)
            tiles = []
            for k in range(OCH):
                tiles.append((gatew, gdq_d, 128 * k, 0))
            for k in range(OCH):
                tiles.append((upw, udq_d, 128 * k, 0))
            for rt in range(CH):
                for ct in range(3):
                    tiles.append((downw, ddq_d, 128 * rt, 1024 * ct))
            half_store_mark = {0: 0, 1: 0}
            for ti_, (src, dst, r0, c0) in enumerate(tiles):
                hb = ti_ % 2
                w_sl = BF1[:, 2048 * hb:2048 * hb + 1024]
                bm_sl = bm[:, :] if hb == 0 else bi[:, :]
                dq_sl = BB1[:, 2048 * hb:2048 * hb + 1024]
                lsem = "lw1" if hb == 0 else "lw2"
                ssem = "sw0" if hb == 0 else "sw1"
                S.wait(SY, ssem, half_store_mark[hb])
                S.do(SY, lambda e, src=src, r0=r0, c0=c0, w_sl=w_sl: e.dma_start(
                    out=w_sl, in_=src[r0:r0 + 128, c0:c0 + 1024]), sem=lsem)
                S.wait(DVE, lsem)
                S.wait(DVE, ssem, half_store_mark[hb])
                S.do(DVE, lambda e, w_sl=w_sl, bm_sl=bm_sl: e.tensor_reduce(
                    bm_sl, w_sl.rearrange("p (b k) -> p b k", k=32),
                    axis=mybir.AxisListType.X, op=ALU.max,
                    apply_absolute_value=True))
                S.do(DVE, lambda e, bm_sl=bm_sl: e.tensor_scalar(
                    bm_sl, bm_sl, 1.0 / 7.0, 1e-12, op0=ALU.mult, op1=ALU.max))
                S.do(DVE, lambda e, bm_sl=bm_sl: e.reciprocal(
                    stat[:, 0:32], bm_sl))
                S.do(DVE, lambda e, bm_sl=bm_sl: e.tensor_tensor(
                    stat[:, 32:64], stat[:, 0:32], bm_sl, op=ALU.mult))
                S.do(DVE, lambda e: e.tensor_scalar(
                    stat[:, 32:64], stat[:, 32:64], -1.0, 2.0,
                    op0=ALU.mult, op1=ALU.add))
                S.do(DVE, lambda e: e.tensor_tensor(
                    stat[:, 0:32], stat[:, 0:32], stat[:, 32:64], op=ALU.mult))
                S.do(DVE, lambda e, w_sl=w_sl: e.tensor_tensor(
                    OST[:, 0:1024].rearrange("p (b k) -> p b k", k=32),
                    w_sl.rearrange("p (b k) -> p b k", k=32),
                    stat[:, 0:32].unsqueeze(2).broadcast_to((128, 32, 32)),
                    op=ALU.mult))
                S.do(DVE, lambda e: e.tensor_scalar(
                    OST[:, 1024:2048], OST[:, 0:1024], 8.0, None, op0=ALU.add))
                S.do(DVE, lambda e: e.tensor_copy(BIGI[:, :], OST[:, 1024:2048]))
                S.wait(ACT, "dve")
                S.do(ACT, lambda e: e.activation(
                    OST[:, 0:1024], BIGI[:, :], AF.Copy, bias=0.0, scale=1.0))
                S.wait(DVE, "act")
                S.do(DVE, lambda e, bm_sl=bm_sl, dq_sl=dq_sl: e.scalar_tensor_tensor(
                    dq_sl.rearrange("p (b k) -> p b k", k=32),
                    OST[:, 0:1024].rearrange("p (b k) -> p b k", k=32), -8.0,
                    bm_sl.unsqueeze(2).broadcast_to((128, 32, 32)),
                    op0=ALU.add, op1=ALU.mult))
                S.wait(GP, "dve")
                S.do(GP, lambda e, dst=dst, r0=r0, c0=c0, dq_sl=dq_sl: e.dma_start(
                    out=dst[r0:r0 + 128, c0:c0 + 1024], in_=dq_sl), sem=ssem)
                half_store_mark[hb] = S.mark(ssem)
            wdq_st0, wdq_st1 = S.mark("sw0"), S.mark("sw1")

            def ln_stats(src_bf_tile, sq_bf_tile):
                for c in range(CH):
                    S.do(PE, lambda e, c=c: e.matmul(
                        p_sum[:, :], ones[:], src_bf_tile(c),
                        start=(c == 0), stop=(c == CH - 1)))
                for c in range(CH):
                    S.do(PE, lambda e, c=c: e.matmul(
                        p_sq[:, :], ones[:], sq_bf_tile(c),
                        start=(c == 0), stop=(c == CH - 1)))
                S.wait(DVE, "pe")
                S.do(DVE, lambda e: e.tensor_scalar(
                    stat[:, 0:ST], p_sum[:, :], 1.0 / C, None, op0=ALU.mult))
                S.do(DVE, lambda e: e.tensor_tensor(
                    OST[:, 0:ST], stat[:, 0:ST], stat[:, 0:ST], op=ALU.mult))
                S.do(DVE, lambda e: e.scalar_tensor_tensor(
                    stat[:, ST:2 * ST], p_sq[:, :], 1.0 / C,
                    OST[:, 0:ST], op0=ALU.mult, op1=ALU.subtract))
                S.wait(ACT, "dve")
                S.do(ACT, lambda e: e.activation(
                    stat[:, ST:2 * ST], stat[:, ST:2 * ST], AF.Ln,
                    bias=scl[:, 24:25], scale=1.0))
                S.do(ACT, lambda e: e.activation(
                    stat[:, ST:2 * ST], stat[:, ST:2 * ST], AF.Exp,
                    bias=0.0, scale=scl[:, 25:26]))
                S.wait(DVE, "act")

            for s in range(NS):
                tok0 = ST * s
                if s == 0:
                    S.wait(SY, "sw0", wdq_st0)
                    S.wait(SY, "sw1", wdq_st1)
                S.wait(SY, "sr1")
                for c in range(CH):
                    S.do(SY, lambda e, c=c: e.dma_start(
                        out=BF1[:, ST * c:ST * (c + 1)],
                        in_=x_t[128 * c:128 * (c + 1), tok0:tok0 + ST]), sem="lx")
                x_ld = S.mark("lx")
                S.wait(ACT, "lx", x_ld)
                if s == 0:
                    S.wait(ACT, "dve", consts_dve)
                S.wait(ACT, "stt")
                if s == 0:
                    S.wait(ACT, "sw0", wdq_st0)
                    S.wait(ACT, "sw1", wdq_st1)
                for c in range(CH):
                    S.do(ACT, lambda e, c=c: e.activation(
                        BB1[:, ST * c:ST * (c + 1)], BF1[:, ST * c:ST * (c + 1)],
                        AF.Copy, bias=0.0, scale=1.0))
                    S.do(ACT, lambda e, c=c: e.activation(
                        BB2[:, ST * c:ST * (c + 1)], BF1[:, ST * c:ST * (c + 1)],
                        AF.Square, bias=0.0, scale=1.0))
                S.wait(PE, "act")
                if s == 0:
                    S.wait(PE, "dve", consts_dve)
                ln_stats(lambda c: BB1[:, ST * c:ST * (c + 1)],
                         lambda c: BB2[:, ST * c:ST * (c + 1)])
                for c in range(CH):
                    S.do(DVE, lambda e, c=c: e.tensor_tensor(
                        OST[:, 0:ST], BF1[:, ST * c:ST * (c + 1)],
                        stat[:, 0:ST], op=ALU.subtract))
                    S.do(DVE, lambda e, c=c: e.tensor_tensor(
                        BB1[:, ST * c:ST * (c + 1)], OST[:, 0:ST],
                        stat[:, ST:2 * ST], op=ALU.mult))
                ln1_dve = S.mark("dve")

                S.wait(PE, "dve", ln1_dve)
                if s == 0:
                    S.wait(PE, "lsel", sel_ld)
                qm = {}
                for oc in range(CH):
                    pp = p_mm if (oc % 2 == 0) else p_sq
                    if oc >= 2:
                        S.wait(PE, "act", qm[oc % 2])
                    for k in range(CH):
                        S.do(PE, lambda e, oc=oc, k=k, pp=pp: e.matmul(
                            pp[:, :], W1[:, 3072 * k + 128 * oc:3072 * k + 128 * (oc + 1)],
                            BB1[:, ST * k:ST * (k + 1)],
                            start=(k == 0), stop=(k == CH - 1)))
                    S.wait(ACT, "pe")
                    S.do(ACT, lambda e, oc=oc, pp=pp: e.activation(
                        BB2[:, ST * oc:ST * (oc + 1)], pp[:, :], AF.Exp,
                        bias=0.0, scale=1.0))
                    qm[oc % 2] = S.mark("act")
                qe_act = S.mark("act")
                S.wait(PE, "act", qe_act)
                for c in range(CH):
                    S.do(PE, lambda e, c=c: e.matmul(
                        p_bc[0:16, :], sel[:, 16 * c:16 * (c + 1)],
                        BB2[:, ST * c:ST * (c + 1)],
                        start=(c == 0), stop=(c == CH - 1)))
                S.wait(ACT, "pe")
                S.do(ACT, lambda e: e.activation(
                    serb[:, :], p_bc[0:16, :], AF.Ln, bias=0.0,
                    scale=scl[0:16, 26:27]))
                S.do(ACT, lambda e: e.activation(
                    serb[:, :], serb[:, :], AF.Exp, bias=0.0,
                    scale=scl[0:16, 27:28]))
                bcm = {}
                for c in range(CH):
                    pb = p_bc if (c % 2 == 0) else p_mm
                    rb = rbcb[:, 0:256].rearrange("p x -> p x") if False else None
                    if c < 2:
                        S.wait(PE, "act")
                    else:
                        S.wait(PE, "act", bcm[c % 2])
                    S.do(PE, lambda e, c=c, pb=pb: e.matmul(
                        pb[:, :], selT[:, 128 * c:128 * (c + 1)], serb[:, :],
                        start=True, stop=True))
                    S.wait(ACT, "pe")
                    S.wait(ACT, "dve")
                    S.do(ACT, lambda e, pb=pb: e.activation(
                        rbcb[:, :], pb[:, :], AF.Copy, bias=0.0, scale=1.0))
                    bcm[c % 2] = S.mark("act")
                    S.wait(DVE, "act")
                    S.do(DVE, lambda e, c=c: e.tensor_tensor(
                        BB2[:, ST * c:ST * (c + 1)], BB2[:, ST * c:ST * (c + 1)],
                        rbcb[:, :], op=ALU.mult))
                qs_dve = S.mark("dve")

                S.wait(PE, "dve", qs_dve)
                kvm = {}
                for tb in range(4):
                    for nch in range(4):
                        pp = p_mm if (nch % 2 == 0) else p_sq
                        if 4 * tb + nch >= 2:
                            S.wait(PE, "act", kvm[nch % 2])
                        for k in range(CH):
                            S.do(PE, lambda e, tb=tb, nch=nch, k=k, pp=pp: e.matmul(
                                pp[:, :],
                                BB1[:, ST * k + 128 * tb:ST * k + 128 * (tb + 1)],
                                W1[:, 3072 * k + 1024 + 512 * nch:3072 * k + 1024 + 512 * (nch + 1)],
                                start=(k == 0), stop=(k == CH - 1)))
                        S.wait(ACT, "pe")
                        if nch < 2:
                            S.do(ACT, lambda e, tb=tb, nch=nch, pp=pp: e.activation(
                                BB3[:, 1024 * tb + 512 * nch:1024 * tb + 512 * (nch + 1)],
                                pp[:, :], AF.Exp, bias=0.0, scale=1.0))
                        else:
                            S.do(ACT, lambda e, tb=tb, nch=nch, pp=pp: e.activation(
                                BB4[:, 1024 * tb + 512 * (nch - 2):1024 * tb + 512 * (nch - 1)],
                                pp[:, :], AF.Copy, bias=0.0, scale=1.0))
                        kvm[nch % 2] = S.mark("act")
                kv_act = S.mark("act")

                S.wait(PE, "act", kv_act)
                for ub in range(8):
                    u = 8 * s + ub
                    tb, half = ub // 2, ub % 2
                    toff = 128 * tb + 64 * half
                    S.wait(PE, "dve")
                    for q_ in range(CH):
                        S.do(PE, lambda e, q_=q_, tb=tb, half=half: e.matmul(
                            p_S[:, 128 * q_:128 * (q_ + 1)],
                            BB3[64 * half:64 * half + 64, 1024 * tb + 128 * q_:1024 * tb + 128 * (q_ + 1)],
                            BB4[64 * half:64 * half + 64, 1024 * tb + 128 * q_:1024 * tb + 128 * (q_ + 1)],
                            start=True, stop=True))
                        S.do(PE, lambda e, q_=q_, tb=tb, half=half: e.matmul(
                            p_bc[:, q_:q_ + 1],
                            BB3[64 * half:64 * half + 64, 1024 * tb + 128 * q_:1024 * tb + 128 * (q_ + 1)],
                            onecol[64 * half:64 * half + 64, :],
                            start=True, stop=True))
                    S.wait(PE, "act")
                    if u > 0:
                        for q_ in range(CH):
                            S.do(PE, lambda e, q_=q_, toff=toff: e.matmul(
                                p_att[:, 64 * q_:64 * (q_ + 1)],
                                Sb[:, 128 * q_:128 * (q_ + 1)],
                                BB2[:, ST * q_ + toff:ST * q_ + toff + 64],
                                start=True, stop=True))
                            S.do(PE, lambda e, q_=q_, toff=toff: e.matmul(
                                p_att[0:2, 512 + 64 * q_:512 + 64 * (q_ + 1)],
                                zbd[:, 2 * q_:2 * q_ + 2],
                                BB2[:, ST * q_ + toff:ST * q_ + toff + 64],
                                start=True, stop=True))
                    blk_pe = S.mark("pe")
                    S.wait(DVE, "pe", blk_pe)
                    if u > 0:
                        S.do(DVE, lambda e, ub=ub: e.tensor_copy(
                            d2.rearrange("p (q t) -> p q t", q=CH)[:, :, 64 * ub:64 * ub + 64],
                            p_att[0:2, 512:1024].rearrange("p (q t) -> p q t", q=CH)))
                        S.do(DVE, lambda e, toff=toff: e.tensor_copy(
                            ARB[:].rearrange("p (c t) -> p c t", c=CH)[:, :, toff:toff + 64],
                            p_att[:, 0:512].rearrange("p (c t) -> p c t", c=CH)))
                    else:
                        S.do(DVE, lambda e: e.memset(
                            d2.rearrange("p (q t) -> p q t", q=CH)[:, :, 0:64], 0.0))
                        S.do(DVE, lambda e: e.memset(
                            ARB[:].rearrange("p (c t) -> p c t", c=CH)[:, :, 0:64], 0.0))
                    S.do(DVE, lambda e: e.tensor_tensor(
                        Sf[:], Sf[:], p_S[:], op=ALU.add))
                    S.do(DVE, lambda e: e.tensor_tensor(
                        zf[:], zf[:], p_bc[:, 0:8], op=ALU.add))
                    blk_dve = S.mark("dve")
                    S.wait(ACT, "dve", blk_dve)
                    for j in range(2):
                        S.do(ACT, lambda e, j=j: e.activation(
                            Sb.rearrange("p (q j x) -> p q j x", q=CH, j=2)[64 * j:64 * j + 64, :, j, :],
                            Sf.rearrange("p (q j x) -> p q j x", q=CH, j=2)[64 * j:64 * j + 64, :, j, :],
                            AF.Copy, bias=0.0, scale=1.0))
                        S.do(ACT, lambda e, j=j: e.activation(
                            zbd.rearrange("p (q j) -> p q j", q=CH)[64 * j:64 * j + 64, :, j],
                            zf[64 * j:64 * j + 64, :],
                            AF.Copy, bias=0.0, scale=1.0))

                S.wait(DVE, "act")
                S.do(DVE, lambda e: e.tensor_scalar(
                    d2[:, :], d2[:, :], ATT_EPS, None, op0=ALU.max))
                S.wait(ACT, "dve")
                S.do(ACT, lambda e: e.activation(
                    d2[:, :], d2[:, :], AF.Ln, bias=0.0, scale=1.0))
                S.do(ACT, lambda e: e.activation(
                    d2[:, :], d2[:, :], AF.Exp, bias=0.0, scale=scl[0:2, 27:28]))
                dbm = {}
                for c in range(CH):
                    pb = p_bc if (c % 2 == 0) else p_mm
                    if c < 2:
                        S.wait(PE, "act")
                    else:
                        S.wait(PE, "act", dbm[c % 2])
                    S.do(PE, lambda e, c=c, pb=pb: e.matmul(
                        pb[:, :], sel2[:, :], d2[:, 512 * c:512 * (c + 1)],
                        start=True, stop=True))
                    S.wait(ACT, "pe")
                    S.wait(ACT, "dve")
                    S.do(ACT, lambda e, pb=pb: e.activation(
                        rbcb[:, :], pb[:, :], AF.Copy, bias=0.0, scale=1.0))
                    dbm[c % 2] = S.mark("act")
                    S.wait(DVE, "act")
                    S.do(DVE, lambda e, c=c: e.tensor_tensor(
                        ARB[:, ST * c:ST * (c + 1)], ARB[:, ST * c:ST * (c + 1)],
                        rbcb[:, :], op=ALU.mult))
                ab_dve = S.mark("dve")

                S.wait(PE, "dve", ab_dve)
                pjm = {}
                for oc in range(CH):
                    pp = p_mm if (oc % 2 == 0) else p_sq
                    if oc >= 2:
                        S.wait(PE, "act", pjm[oc % 2])
                    for k in range(CH):
                        S.do(PE, lambda e, oc=oc, k=k, pp=pp: e.matmul(
                            pp[:, :], W2a[:, 1024 * k + 128 * oc:1024 * k + 128 * (oc + 1)],
                            ARB[:, ST * k:ST * (k + 1)],
                            start=(k == 0), stop=(k == CH - 1)))
                    S.wait(ACT, "pe")
                    if s == 0 and oc == 0:
                        S.wait(ACT, "lsc", projb_ld)
                    S.do(ACT, lambda e, oc=oc, pp=pp: e.activation(
                        BB2[:, ST * oc:ST * (oc + 1)], pp[:, :], AF.Identity,
                        bias=projb[:, oc:oc + 1], scale=1.0))
                    S.do(ACT, lambda e, oc=oc, pp=pp: e.activation(
                        BB1[:, ST * oc:ST * (oc + 1)], pp[:, :], AF.Square,
                        bias=projb[:, oc:oc + 1], scale=1.0))
                    pjm[oc % 2] = S.mark("act")
                S.wait(PE, "act")
                ln_stats(lambda c: BB2[:, ST * c:ST * (c + 1)],
                         lambda c: BB1[:, ST * c:ST * (c + 1)])
                for h2 in range(2):
                    for c in range(CH):
                        off = ST * c + 256 * h2
                        S.do(ACT, lambda e, c=c, off=off: e.activation(
                            OST[:, 256 * c:256 * (c + 1)], BB2[:, off:off + 256],
                            AF.Copy, bias=0.0, scale=1.0))
                    S.wait(DVE, "act")
                    for c in range(CH):
                        off = ST * c + 256 * h2
                        S.do(DVE, lambda e, c=c, h2=h2: e.tensor_tensor(
                            OST[:, 256 * c:256 * (c + 1)], OST[:, 256 * c:256 * (c + 1)],
                            stat[:, 256 * h2:256 * h2 + 256], op=ALU.subtract))
                        S.do(DVE, lambda e, c=c, h2=h2: e.tensor_tensor(
                            OST[:, 256 * c:256 * (c + 1)], OST[:, 256 * c:256 * (c + 1)],
                            stat[:, ST + 256 * h2:ST + 256 * h2 + 256], op=ALU.mult))
                        S.do(DVE, lambda e, c=c, off=off: e.tensor_tensor(
                            BF1[:, off:off + 256], BF1[:, off:off + 256],
                            OST[:, 256 * c:256 * (c + 1)], op=ALU.add))
                    S.wait(ACT, "dve")
                r1_dve = S.mark("dve")
                S.wait(GP, "dve", r1_dve)
                for c in range(CH):
                    S.do(GP, lambda e, c=c: e.dma_start(
                        out=r1_d[128 * c:128 * (c + 1), tok0:tok0 + ST],
                        in_=BF1[:, ST * c:ST * (c + 1)]), sem="sr1")

                S.wait(ACT, "dve", r1_dve)
                for c in range(CH):
                    S.do(ACT, lambda e, c=c: e.activation(
                        BB1[:, ST * c:ST * (c + 1)], BF1[:, ST * c:ST * (c + 1)],
                        AF.Copy, bias=0.0, scale=1.0))
                    S.do(ACT, lambda e, c=c: e.activation(
                        BB2[:, ST * c:ST * (c + 1)], BF1[:, ST * c:ST * (c + 1)],
                        AF.Square, bias=0.0, scale=1.0))
                S.wait(PE, "act")
                ln_stats(lambda c: BB1[:, ST * c:ST * (c + 1)],
                         lambda c: BB2[:, ST * c:ST * (c + 1)])
                for h2 in range(2):
                    for c in range(CH):
                        off = ST * c + 256 * h2
                        S.do(DVE, lambda e, c=c, off=off, h2=h2: e.tensor_tensor(
                            OST[:, 256 * c:256 * (c + 1)], BF1[:, off:off + 256],
                            stat[:, 256 * h2:256 * h2 + 256], op=ALU.subtract))
                        S.do(DVE, lambda e, c=c, off=off, h2=h2: e.tensor_tensor(
                            BB1[:, off:off + 256], OST[:, 256 * c:256 * (c + 1)],
                            stat[:, ST + 256 * h2:ST + 256 * h2 + 256], op=ALU.mult))
                ln3_dve = S.mark("dve")
                S.do(DVE, lambda e: e.tensor_reduce(
                    mxacc[:, 5:6], BB1[:, :], axis=mybir.AxisListType.X,
                    op=ALU.max, apply_absolute_value=True))
                S.do(DVE, lambda e: e.tensor_tensor(
                    mxacc[:, 0:1], mxacc[:, 0:1], mxacc[:, 5:6], op=ALU.max))
                S.wait(GP, "dve", S.mark("dve"))
                for c in range(CH):
                    S.do(GP, lambda e, c=c: e.dma_start(
                        out=t_d[128 * c:128 * (c + 1), tok0:tok0 + ST],
                        in_=BB1[:, ST * c:ST * (c + 1)]), sem="stt")
                S.wait(SY, "dve", ln3_dve)
            phA_pe = S.mark("pe")
            phA_sr1, phA_stt = S.mark("sr1"), S.mark("stt")

            def global_max(src_col, n_scal, dst_col):
                for q_ in range(2):
                    sc = src_col + min(q_, n_scal - 1)
                    S.wait(SY, "dve")
                    S.do(SY, lambda e, sc=sc: e.dma_start(
                        out=mx_d.ap().rearrange("o p -> p o"),
                        in_=mxacc[:, sc:sc + 1]),
                        sem="lsc")
                    S.wait(SY, "lsc")
                    S.do(SY, lambda e: e.dma_start(
                        out=rowred[:, :], in_=mx_d[:, :]), sem="lsc")
                    S.wait(DVE, "lsc")
                    S.do(DVE, lambda e: e.tensor_reduce(
                        rowred[0:1, 0:1], rowred[0:1, 0:128],
                        axis=mybir.AxisListType.X, op=ALU.max,
                        apply_absolute_value=False))
                    S.wait(SY, "dve")
                    S.do(SY, lambda e, q_=q_: e.dma_start(
                        out=cc_in[0:1, q_:q_ + 1], in_=rowred[0:1, 0:1]), sem="lsc")
                S.wait(GP, "lsc")
                S.do(GP, lambda e: e.collective_compute(
                    "AllReduce", ALU.max,
                    replica_groups=[list(range(N_CORES))],
                    ins=[cc_in.ap()], outs=[cc_out.ap()]), sem="cc")
                S.wait(SY, "cc")
                S.do(SY, lambda e, dst_col=dst_col, n_scal=n_scal: e.dma_start(
                    out=scl[:, dst_col:dst_col + n_scal],
                    in_=cc_out[0:1, 0:n_scal].broadcast_to((128, n_scal))),
                    sem="lsc")
                S.wait(DVE, "lsc")

            S.wait(SY, "sw0", wdq_st0)
            S.wait(SY, "sw1", wdq_st1)
            S.wait(SY, "pe", phA_pe)
            for c in range(CH):
                S.do(SY, lambda e, c=c: e.dma_start_transpose(
                    W1[:, 3072 * c:3072 * (c + 1)], gdq_d[:, 128 * c:128 * (c + 1)]),
                    sem="lwT")
            for c in range(CH):
                f = 3072 * c
                if f + 3072 <= 8192:
                    S.do(SY, lambda e, c=c, f=f: e.dma_start_transpose(
                        W2a[:, f:f + 3072], udq_d[:, 128 * c:128 * (c + 1)]),
                        sem="lwT")
                elif f >= 8192:
                    S.do(SY, lambda e, c=c, f=f: e.dma_start_transpose(
                        W2b[:, f - 8192:f - 8192 + 3072],
                        udq_d[:, 128 * c:128 * (c + 1)]), sem="lwT")
                else:
                    lo = 8192 - f
                    S.do(SY, lambda e, c=c, f=f, lo=lo: e.dma_start_transpose(
                        W2a[:, f:8192], udq_d[0:lo, 128 * c:128 * (c + 1)]),
                        sem="lwT")
                    S.do(SY, lambda e, c=c, lo=lo: e.dma_start_transpose(
                        W2b[:, 0:3072 - lo], udq_d[lo:3072, 128 * c:128 * (c + 1)]),
                        sem="lwT")
            ffnw_ld = S.mark("lwT")

            global_max(0, 1, 0)
            S.do(DVE, lambda e: e.tensor_scalar(
                scl[:, sT:sT + 1], scl[:, 0:1], 1.0 / QMAX, 1e-12,
                op0=ALU.mult, op1=ALU.max))
            S.do(DVE, lambda e: e.reciprocal(scl[:, iT:iT + 1], scl[:, sT:sT + 1]))
            S.do(DVE, lambda e: e.tensor_tensor(
                scl[:, 30:31], scl[:, iT:iT + 1], scl[:, sT:sT + 1], op=ALU.mult))
            S.do(DVE, lambda e: e.tensor_scalar(
                scl[:, 30:31], scl[:, 30:31], -1.0, 2.0, op0=ALU.mult, op1=ALU.add))
            S.do(DVE, lambda e: e.tensor_tensor(
                scl[:, iT:iT + 1], scl[:, iT:iT + 1], scl[:, 30:31], op=ALU.mult))
            ar1_dve = S.mark("dve")

            pair_store_mark = {0: 0, 1: 0}
            for s in range(NS):
                tok0 = ST * s
                S.wait(SY, "act")
                S.wait(SY, "stt", phA_stt)
                for c in range(CH):
                    S.do(SY, lambda e, c=c: e.dma_start(
                        out=BB1[:, ST * c:ST * (c + 1)],
                        in_=t_d[128 * c:128 * (c + 1), tok0:tok0 + ST]), sem="lt")
                t_ld = S.mark("lt")
                S.wait(ACT, "lt", t_ld)
                S.wait(ACT, "dve", ar1_dve)
                S.wait(DVE, "pe")
                for c in range(CH):
                    S.do(ACT, lambda e, c=c: e.activation(
                        OST[:, 0:ST], BB1[:, ST * c:ST * (c + 1)], AF.Identity,
                        bias=scl[:, 28:29], scale=scl[:, iT:iT + 1]))
                    S.wait(DVE, "act")
                    S.do(DVE, lambda e: e.tensor_copy(BIGI[:, 0:ST], OST[:, 0:ST]))
                    S.do(DVE, lambda e, c=c: e.tensor_scalar(
                        BB2[:, ST * c:ST * (c + 1)], BIGI[:, 0:ST], 32768.0, None,
                        op0=ALU.subtract))
                    S.wait(ACT, "dve")
                tq_dve = S.mark("dve")
                S.wait(PE, "dve", tq_dve)
                if s == 0:
                    S.wait(PE, "lwT", ffnw_ld)
                f1m = {}
                for which, dst, mxcol in ((0, graw_d, 1), (1, uraw_d, 2)):
                    for oc in range(OCH):
                        pp = p_mm if (oc % 2 == 0) else p_sq
                        slot = oc % 4
                        pair = slot // 2
                        jdx = 24 * which + oc
                        if jdx >= 2:
                            S.wait(PE, "act", f1m[jdx % 2])
                        for k in range(CH):
                            lhs = (W1[:, 3072 * k + 128 * oc:3072 * k + 128 * (oc + 1)]
                                   if which == 0 else uT_sl(k, oc))
                            S.do(PE, lambda e, k=k, pp=pp, lhs=lhs: e.matmul(
                                pp[:, :], lhs, BB2[:, ST * k:ST * (k + 1)],
                                start=(k == 0), stop=(k == CH - 1)))
                        S.wait(ACT, "pe")
                        S.wait(ACT, "sf0" if pair == 0 else "sf1",
                               pair_store_mark[pair])
                        S.wait(ACT, "dve")
                        S.do(ACT, lambda e, slot=slot, pp=pp: e.activation(
                            fst[:, 512 * slot:512 * slot + 512],
                            pp[:, :], AF.Copy, bias=0.0, scale=1.0))
                        S.wait(DVE, "act")
                        S.do(DVE, lambda e, slot=slot: e.tensor_reduce(
                            mxacc[:, 5:6], fst[:, 512 * slot:512 * slot + 512],
                            axis=mybir.AxisListType.X, op=ALU.max,
                            apply_absolute_value=True))
                        S.do(DVE, lambda e, mxcol=mxcol: e.tensor_tensor(
                            mxacc[:, mxcol:mxcol + 1], mxacc[:, mxcol:mxcol + 1],
                            mxacc[:, 5:6], op=ALU.max))
                        f1m[jdx % 2] = S.mark("act")
                        if oc % 2 == 1:
                            S.wait(GP, "act")
                            S.do(GP, lambda e, oc=oc, dst=dst, tok0=tok0, pair=pair: e.dma_start(
                                out=dst[128 * (oc - 1):128 * (oc + 1), tok0:tok0 + ST].rearrange(
                                    "(a p) t -> p a t", p=128),
                                in_=fst[:, 1024 * pair:1024 * (pair + 1)].rearrange(
                                    "p (a t) -> p a t", a=2)),
                                sem="sf0" if pair == 0 else "sf1")
                            pair_store_mark[pair] = S.mark(
                                "sf0" if pair == 0 else "sf1")
            f1_pe = S.mark("pe")
            f1_sf0, f1_sf1 = S.mark("sf0"), S.mark("sf1")

            global_max(1, 2, 3)
            S.do(DVE, lambda e: e.tensor_tensor(
                scl[:, sG:sG + 1], scl[:, 3:4], scl[:, sT:sT + 1], op=ALU.mult))
            S.do(DVE, lambda e: e.tensor_scalar(
                scl[:, sG:sG + 1], scl[:, sG:sG + 1], 1.0 / QMAX, 1e-12,
                op0=ALU.mult, op1=ALU.max))
            S.do(DVE, lambda e: e.reciprocal(scl[:, 20:21], scl[:, sG:sG + 1]))
            S.do(DVE, lambda e: e.tensor_tensor(
                scl[:, 30:31], scl[:, 20:21], scl[:, sG:sG + 1], op=ALU.mult))
            S.do(DVE, lambda e: e.tensor_scalar(
                scl[:, 30:31], scl[:, 30:31], -1.0, 2.0, op0=ALU.mult, op1=ALU.add))
            S.do(DVE, lambda e: e.tensor_tensor(
                scl[:, 20:21], scl[:, 20:21], scl[:, 30:31], op=ALU.mult))
            S.do(DVE, lambda e: e.tensor_tensor(
                scl[:, rG:rG + 1], scl[:, sT:sT + 1], scl[:, 20:21], op=ALU.mult))
            S.do(DVE, lambda e: e.tensor_scalar(
                scl[:, nbG:nbG + 1], scl[:, sG:sG + 1], -32768.0, None, op0=ALU.mult))
            S.do(DVE, lambda e: e.tensor_scalar(
                scl[:, sG65:sG65 + 1], scl[:, sG:sG + 1], 1.0 / 65536.0, None,
                op0=ALU.mult))
            S.do(DVE, lambda e: e.tensor_scalar(
                scl[:, nbG65:nbG65 + 1], scl[:, sG65:sG65 + 1], -32768.0, None,
                op0=ALU.mult))
            S.do(DVE, lambda e: e.tensor_tensor(
                scl[:, sU:sU + 1], scl[:, 4:5], scl[:, sT:sT + 1], op=ALU.mult))
            S.do(DVE, lambda e: e.tensor_scalar(
                scl[:, sU:sU + 1], scl[:, sU:sU + 1], 1.0 / QMAX, 1e-12,
                op0=ALU.mult, op1=ALU.max))
            S.do(DVE, lambda e: e.reciprocal(scl[:, 21:22], scl[:, sU:sU + 1]))
            S.do(DVE, lambda e: e.tensor_tensor(
                scl[:, 30:31], scl[:, 21:22], scl[:, sU:sU + 1], op=ALU.mult))
            S.do(DVE, lambda e: e.tensor_scalar(
                scl[:, 30:31], scl[:, 30:31], -1.0, 2.0, op0=ALU.mult, op1=ALU.add))
            S.do(DVE, lambda e: e.tensor_tensor(
                scl[:, 21:22], scl[:, 21:22], scl[:, 30:31], op=ALU.mult))
            S.do(DVE, lambda e: e.tensor_tensor(
                scl[:, rU:rU + 1], scl[:, sT:sT + 1], scl[:, 21:22], op=ALU.mult))
            ar2_dve = S.mark("dve")

            f2_act_mark = {0: 0, 1: 0}
            f2_ldm = {0: 0, 1: 0}
            f2_stm = {0: 0, 1: 0}
            for r in range(OCH + 1):
                if r < OCH:
                    gbuf = BB1 if (r % 2 == 0) else BB3
                    lsem = "lt" if (r % 2 == 0) else "lt2"
                    S.wait(SY, "act", f2_act_mark[r % 2])
                    if r == 0:
                        S.wait(SY, "sf0", f1_sf0)
                        S.wait(SY, "sf1", f1_sf1)
                    S.do(SY, lambda e, r=r, gbuf=gbuf: e.dma_start(
                        out=gbuf[:, 0:2048],
                        in_=graw_d[128 * r:128 * (r + 1), 0:2048]), sem=lsem)
                    S.do(SY, lambda e, r=r, gbuf=gbuf: e.dma_start(
                        out=gbuf[:, 2048:4096],
                        in_=graw_d[128 * r:128 * (r + 1), 2048:4096]), sem=lsem)
                    f2_ldm[r % 2] = S.mark(lsem)
                if r == 0:
                    continue
                r_ = r - 1
                gbuf = BB1 if (r_ % 2 == 0) else BB3
                obuf = ARB if (r_ % 2 == 0) else BB4
                ssem = "sg2" if (r_ % 2 == 0) else "sg3"
                S.wait(ACT, "lt" if (r_ % 2 == 0) else "lt2", f2_ldm[r_ % 2])
                S.wait(ACT, "dve", ar2_dve)
                S.wait(ACT, "sr1")
                for q_ in range(4):
                    sl = slice(1024 * q_, 1024 * (q_ + 1))
                    fa = BF1[:, 0:1024]
                    fb = BF1[:, 1024:2048]
                    fc = BF1[:, 2048:3072]
                    S.wait(ACT, "dve")
                    S.do(ACT, lambda e, sl=sl, fa=fa, gbuf=gbuf: e.activation(
                        fa, gbuf[:, sl], AF.Identity, bias=scl[:, 28:29],
                        scale=scl[:, rG:rG + 1]))
                    S.wait(DVE, "act")
                    S.do(DVE, lambda e, fa=fa: e.tensor_copy(BIGI[:, :], fa))
                    S.wait(ACT, "dve")
                    S.do(ACT, lambda e, fb=fb: e.activation(
                        fb, BIGI[:, :], AF.Sigmoid, bias=scl[:, nbG:nbG + 1],
                        scale=scl[:, sG:sG + 1]))
                    S.do(ACT, lambda e, fa=fa: e.activation(
                        fa, BIGI[:, :], AF.Identity, bias=scl[:, nbG65:nbG65 + 1],
                        scale=scl[:, sG65:sG65 + 1]))
                    S.wait(DVE, "act")
                    S.do(DVE, lambda e, fb=fb: e.tensor_scalar(
                        fb, fb, 65536.0, 65535.49, op0=ALU.mult, op1=ALU.min))
                    S.do(DVE, lambda e, fb=fb: e.tensor_copy(BIGI[:, :], fb))
                    S.do(DVE, lambda e, fc=fc: e.tensor_copy(fc, BIGI[:, :]))
                    S.wait(DVE, ssem, f2_stm[r_ % 2])
                    S.do(DVE, lambda e, sl=sl, fa=fa, fc=fc, obuf=obuf: e.tensor_tensor(
                        obuf[:, sl], fa, fc, op=ALU.mult))
                    S.do(DVE, lambda e, sl=sl, obuf=obuf: e.tensor_reduce(
                        mxacc[:, 5:6], obuf[:, sl], axis=mybir.AxisListType.X,
                        op=ALU.max, apply_absolute_value=True))
                    S.do(DVE, lambda e: e.tensor_tensor(
                        mxacc[:, 3:4], mxacc[:, 3:4], mxacc[:, 5:6], op=ALU.max))
                f2_act_mark[r_ % 2] = S.mark("act")
                S.wait(GP, "dve")
                S.do(GP, lambda e, r_=r_, obuf=obuf: e.dma_start(
                    out=g2_d[128 * r_:128 * (r_ + 1), 0:2048], in_=obuf[:, 0:2048]),
                    sem=ssem)
                S.do(GP, lambda e, r_=r_, obuf=obuf: e.dma_start(
                    out=g2_d[128 * r_:128 * (r_ + 1), 2048:4096], in_=obuf[:, 2048:4096]),
                    sem=ssem)
                f2_stm[r_ % 2] = S.mark(ssem)
                S.wait(SY, "pe", f1_pe)
                S.do(SY, lambda e, r_=r_: e.dma_start_transpose(
                    W1[:, 1024 * r_:1024 * (r_ + 1)], ddq_d[:, 128 * r_:128 * (r_ + 1)]),
                    sem="ld2")
            dT_ld = S.mark("ld2")

            global_max(3, 1, 12)
            S.do(DVE, lambda e: e.tensor_scalar(
                scl[:, s2:s2 + 1], scl[:, 12:13], 1.0 / QMAX, 1e-12,
                op0=ALU.mult, op1=ALU.max))
            S.do(DVE, lambda e: e.reciprocal(scl[:, r2:r2 + 1], scl[:, s2:s2 + 1]))
            S.do(DVE, lambda e: e.tensor_tensor(
                scl[:, 30:31], scl[:, r2:r2 + 1], scl[:, s2:s2 + 1], op=ALU.mult))
            S.do(DVE, lambda e: e.tensor_scalar(
                scl[:, 30:31], scl[:, 30:31], -1.0, 2.0, op0=ALU.mult, op1=ALU.add))
            S.do(DVE, lambda e: e.tensor_tensor(
                scl[:, r2:r2 + 1], scl[:, r2:r2 + 1], scl[:, 30:31], op=ALU.mult))
            S.do(DVE, lambda e: e.tensor_tensor(
                scl[:, s2su:s2su + 1], scl[:, s2:s2 + 1], scl[:, sU:sU + 1],
                op=ALU.mult))
            S.do(DVE, lambda e: e.tensor_scalar(
                scl[:, nb2su:nb2su + 1], scl[:, s2su:s2su + 1], -32768.0, None,
                op0=ALU.mult))
            ar3_dve = S.mark("dve")

            f3_act_mark = {0: 0, 1: 0}
            f3_ldm = {0: 0, 1: 0}
            f3_stm = {0: 0, 1: 0}
            for r in range(OCH + 1):
                if r < OCH:
                    gbuf = BB1 if (r % 2 == 0) else BB3
                    ubuf = BB2 if (r % 2 == 0) else BB4
                    lsg = "lt" if (r % 2 == 0) else "lt2"
                    lsu = "lu" if (r % 2 == 0) else "lu2"
                    S.wait(SY, "act", f3_act_mark[r % 2])
                    if r == 0:
                        S.wait(SY, "sg2")
                        S.wait(SY, "sg3")
                    S.do(SY, lambda e, r=r, gbuf=gbuf: e.dma_start(
                        out=gbuf[:, 0:2048],
                        in_=g2_d[128 * r:128 * (r + 1), 0:2048]), sem=lsg)
                    S.do(SY, lambda e, r=r, gbuf=gbuf: e.dma_start(
                        out=gbuf[:, 2048:4096],
                        in_=g2_d[128 * r:128 * (r + 1), 2048:4096]), sem=lsg)
                    S.do(SY, lambda e, r=r, ubuf=ubuf: e.dma_start(
                        out=ubuf[:, 0:2048],
                        in_=uraw_d[128 * r:128 * (r + 1), 0:2048]), sem=lsu)
                    S.do(SY, lambda e, r=r, ubuf=ubuf: e.dma_start(
                        out=ubuf[:, 2048:4096],
                        in_=uraw_d[128 * r:128 * (r + 1), 2048:4096]), sem=lsu)
                    f3_ldm[r % 2] = (S.mark(lsg), S.mark(lsu))
                if r == 0:
                    continue
                r_ = r - 1
                gbuf = BB1 if (r_ % 2 == 0) else BB3
                ubuf = BB2 if (r_ % 2 == 0) else BB4
                obuf = ARB
                ssem = "so" if (r_ % 2 == 0) else "so2"
                S.wait(ACT, "lt" if (r_ % 2 == 0) else "lt2", f3_ldm[r_ % 2][0])
                S.wait(ACT, "lu" if (r_ % 2 == 0) else "lu2", f3_ldm[r_ % 2][1])
                S.wait(ACT, "dve", ar3_dve)
                for q_ in range(4):
                    sl = slice(1024 * q_, 1024 * (q_ + 1))
                    fa = BF1[:, 0:1024]
                    fb = BF1[:, 1024:2048]
                    S.wait(ACT, "dve")
                    S.do(ACT, lambda e, sl=sl, fa=fa, gbuf=gbuf: e.activation(
                        fa, gbuf[:, sl], AF.Identity, bias=scl[:, 28:29],
                        scale=scl[:, r2:r2 + 1]))
                    S.wait(DVE, "act")
                    S.do(DVE, lambda e, fa=fa: e.tensor_copy(BIGI[:, :], fa))
                    S.wait(ACT, "dve")
                    S.do(ACT, lambda e, fa=fa: e.activation(
                        fa, BIGI[:, :], AF.Identity, bias=scl[:, 29:30], scale=1.0))
                    S.do(ACT, lambda e, sl=sl, fb=fb, ubuf=ubuf: e.activation(
                        fb, ubuf[:, sl], AF.Identity, bias=scl[:, 28:29],
                        scale=scl[:, rU:rU + 1]))
                    S.wait(DVE, "act")
                    S.do(DVE, lambda e, fb=fb: e.tensor_copy(BIGI[:, :], fb))
                    S.wait(ACT, "dve")
                    S.do(ACT, lambda e, fb=fb: e.activation(
                        fb, BIGI[:, :], AF.Identity, bias=scl[:, nb2su:nb2su + 1],
                        scale=scl[:, s2su:s2su + 1]))
                    S.wait(DVE, "act")
                    S.wait(DVE, ssem, f3_stm[r_ % 2])
                    S.do(DVE, lambda e, sl=sl, fa=fa, fb=fb, obuf=obuf: e.tensor_tensor(
                        obuf[:, sl], fa, fb, op=ALU.mult))
                    S.do(DVE, lambda e, sl=sl, obuf=obuf: e.tensor_reduce(
                        mxacc[:, 5:6], obuf[:, sl], axis=mybir.AxisListType.X,
                        op=ALU.max, apply_absolute_value=True))
                    S.do(DVE, lambda e: e.tensor_tensor(
                        mxacc[:, 4:5], mxacc[:, 4:5], mxacc[:, 5:6], op=ALU.max))
                f3_act_mark[r_ % 2] = S.mark("act")
                S.wait(GP, "dve")
                S.do(GP, lambda e, r_=r_, obuf=obuf: e.dma_start(
                    out=o_d[128 * r_:128 * (r_ + 1), 0:2048], in_=obuf[:, 0:2048]),
                    sem=ssem)
                S.do(GP, lambda e, r_=r_, obuf=obuf: e.dma_start(
                    out=o_d[128 * r_:128 * (r_ + 1), 2048:4096], in_=obuf[:, 2048:4096]),
                    sem=ssem)
                f3_stm[r_ % 2] = S.mark(ssem)

            global_max(4, 1, 17)
            S.do(DVE, lambda e: e.tensor_scalar(
                scl[:, sO:sO + 1], scl[:, 17:18], 1.0 / QMAX, 1e-12,
                op0=ALU.mult, op1=ALU.max))
            S.do(DVE, lambda e: e.reciprocal(scl[:, rO:rO + 1], scl[:, sO:sO + 1]))
            S.do(DVE, lambda e: e.tensor_tensor(
                scl[:, 30:31], scl[:, rO:rO + 1], scl[:, sO:sO + 1], op=ALU.mult))
            S.do(DVE, lambda e: e.tensor_scalar(
                scl[:, 30:31], scl[:, 30:31], -1.0, 2.0, op0=ALU.mult, op1=ALU.add))
            S.do(DVE, lambda e: e.tensor_tensor(
                scl[:, rO:rO + 1], scl[:, rO:rO + 1], scl[:, 30:31], op=ALU.mult))
            ar4_dve = S.mark("dve")

            S.wait(PE, "ld2", dT_ld)
            S.wait(SY, "so")
            S.wait(SY, "so2")
            S.wait(SY, "sr1", phA_sr1)
            def o_slot(k):
                buf = (BB3, BB4, ARB)[k // 8]
                j = k % 8
                return buf[:, 512 * j:512 * (j + 1)]

            for s in range(NS):
                tok0 = ST * s
                S.wait(DVE, "pe")
                S.wait(SY, "act")
                for k in range(OCH):
                    S.do(SY, lambda e, k=k: e.dma_start(
                        out=o_slot(k),
                        in_=o_d[128 * k:128 * (k + 1), tok0:tok0 + ST]), sem="lo")
                S.wait(ACT, "lo", S.mark("lo"))
                S.wait(ACT, "dve", ar4_dve)
                S.wait(ACT, "dve")
                omark = {}
                for k in range(OCH):
                    if k >= 4:
                        S.wait(ACT, "dve", omark[k % 4])
                    S.do(ACT, lambda e, k=k: e.activation(
                        OST[:, 512 * (k % 4):512 * (k % 4) + 512], o_slot(k),
                        AF.Identity,
                        bias=scl[:, 28:29], scale=scl[:, rO:rO + 1]))
                    S.wait(DVE, "act")
                    S.do(DVE, lambda e, k=k: e.tensor_copy(
                        BIGI[:, 512 * (k % 2):512 * (k % 2) + 512],
                        OST[:, 512 * (k % 4):512 * (k % 4) + 512]))
                    omark[k % 4] = S.mark("dve")
                    S.do(DVE, lambda e, k=k: e.tensor_scalar(
                        oq_sl(k), BIGI[:, 512 * (k % 2):512 * (k % 2) + 512],
                        32768.0, None, op0=ALU.subtract))
                oq_dve = S.mark("dve")
                S.wait(SY, "sout")
                for c in range(CH):
                    S.do(SY, lambda e, c=c: e.dma_start(
                        out=BF1[:, ST * c:ST * (c + 1)],
                        in_=r1_d[128 * c:128 * (c + 1), tok0:tok0 + ST]), sem="lx")
                r1_ld = S.mark("lx")
                S.wait(PE, "dve", oq_dve)
                for oc in range(CH):
                    pp = p_mm if (oc % 2 == 0) else p_sq
                    for k in range(OCH):
                        S.do(PE, lambda e, oc=oc, k=k, pp=pp: e.matmul(
                            pp[:, :], W1[:, 1024 * k + 128 * oc:1024 * k + 128 * (oc + 1)],
                            oq_sl(k), start=(k == 0), stop=(k == OCH - 1)))
                    S.wait(ACT, "pe")
                    S.do(ACT, lambda e, oc=oc, pp=pp: e.activation(
                        BB1[:, ST * oc:ST * (oc + 1)], pp[:, :], AF.Copy,
                        bias=0.0, scale=scl[:, sO:sO + 1]))
                    S.do(ACT, lambda e, oc=oc, pp=pp: e.activation(
                        BB2[:, ST * oc:ST * (oc + 1)], pp[:, :], AF.Square,
                        bias=0.0, scale=scl[:, sO:sO + 1]))
                    S.wait(PE, "act")
                ln_stats(lambda c: BB1[:, ST * c:ST * (c + 1)],
                         lambda c: BB2[:, ST * c:ST * (c + 1)])
                S.wait(DVE, "lx", r1_ld)
                for h2 in range(2):
                    for c in range(CH):
                        off = ST * c + 256 * h2
                        S.do(ACT, lambda e, c=c, off=off: e.activation(
                            OST[:, 256 * c:256 * (c + 1)], BB1[:, off:off + 256],
                            AF.Copy, bias=0.0, scale=1.0))
                    S.wait(DVE, "act")
                    for c in range(CH):
                        off = ST * c + 256 * h2
                        S.do(DVE, lambda e, c=c, h2=h2: e.tensor_tensor(
                            OST[:, 256 * c:256 * (c + 1)], OST[:, 256 * c:256 * (c + 1)],
                            stat[:, 256 * h2:256 * h2 + 256], op=ALU.subtract))
                        S.do(DVE, lambda e, c=c, h2=h2: e.tensor_tensor(
                            OST[:, 256 * c:256 * (c + 1)], OST[:, 256 * c:256 * (c + 1)],
                            stat[:, ST + 256 * h2:ST + 256 * h2 + 256], op=ALU.mult))
                        S.do(DVE, lambda e, c=c, off=off: e.tensor_tensor(
                            BF1[:, off:off + 256], BF1[:, off:off + 256],
                            OST[:, 256 * c:256 * (c + 1)], op=ALU.add))
                    S.wait(ACT, "dve")
                out_dve = S.mark("dve")
                S.wait(GP, "dve", out_dve)
                for c in range(CH):
                    S.do(GP, lambda e, c=c: e.dma_start(
                        out=out_t[128 * c:128 * (c + 1), tok0:tok0 + ST],
                        in_=BF1[:, ST * c:ST * (c + 1)]), sem="sout")

            S.wait(GP, "sout", S.mark("sout"))
            S.wait(SY, "sout", S.mark("sout"))

        @block.sync
        def _(e):
            S.begin(SY, e)
            program()

        @block.tensor
        def _(e):
            S.begin(PE, e)
            program()

        @block.vector
        def _(e):
            S.begin(DVE, e)
            program()

        @block.scalar
        def _(e):
            S.begin(ACT, e)
            program()

        @block.gpsimd
        def _(e):
            S.begin(GP, e)
            program()

    return nc


def kernel(L0_attn_x, L0_attn_qkv_w, L0_attn_proj_w, L0_attn_proj_b,
           L0_ffn_w_gate, L0_ffn_w_up, L0_ffn_w_down):
    x = np.asarray(L0_attn_x, dtype=np.float32)
    qkvw_t = np.ascontiguousarray(np.asarray(L0_attn_qkv_w, np.float32).T)
    projw_t = np.ascontiguousarray(np.asarray(L0_attn_proj_w, np.float32).T)
    projb_t = np.ascontiguousarray(
        np.asarray(L0_attn_proj_b, np.float32).reshape(CH, 128).T)
    gatew = np.ascontiguousarray(np.asarray(L0_ffn_w_gate, np.float32))
    upw = np.ascontiguousarray(np.asarray(L0_ffn_w_up, np.float32))
    downw = np.ascontiguousarray(np.asarray(L0_ffn_w_down, np.float32))

    nc = build_nc()
    in_maps = []
    for b in range(N_CORES):
        in_maps.append(dict(
            x_t=np.ascontiguousarray(x[b].T),
            qkvw_t=qkvw_t, projw_t=projw_t, projb_t=projb_t,
            gatew=gatew, upw=upw, downw=downw,
        ))
    res = run_bass_kernel_spmd(nc, in_maps, core_ids=list(range(N_CORES)))
    out = np.stack([np.asarray(res.results[b]["out_t"], np.float32).T
                    for b in range(N_CORES)], axis=0)
    return np.ascontiguousarray(out)


if __name__ == "__main__":
    nc = build_nc()
    print("build ok")


# version 09
# speedup vs baseline: 1.0079x; 1.0079x over previous
import numpy as np
import concourse.bass as bass
import concourse.mybir as mybir
from concourse.bass_utils import run_bass_kernel_spmd
from contextlib import ExitStack

F32 = mybir.dt.float32
BF16 = mybir.dt.bfloat16
I32 = mybir.dt.int32
AF = mybir.ActivationFunctionType
ALU = mybir.AluOpType

N_CORES = 8
N = 4096
C = 1024
H = 16
E = 64
BS = 64
I3 = 3072
ST = 512
NS = N // ST
CH = C // 128
OCH = I3 // 128
LN_EPS = 1e-5
ATT_EPS = 1e-3
QMAX = 32767.0

PE, ACT, DVE, GP, SY = "pe", "act", "dve", "gp", "sy"
ENG_SEM = {PE: "pe", ACT: "act", DVE: "dve", GP: "gp"}


class Sched:

    def __init__(self, sems):
        self.sems = sems
        self.cnt = {k: 0 for k in sems}
        self.last_wait = {}
        self.cur = None
        self.eng = None

    def begin(self, name, eng):
        self.cur = name
        self.eng = eng
        self.cnt = {k: 0 for k in self.sems}
        self.last_wait = {}

    def do(self, eng_name, fn, sem=None, inc=None):
        if sem is None:
            sem = ENG_SEM[eng_name]
        if inc is None:
            inc = 1 if sem in ("pe", "act", "dve", "gp", "cc") else 16
        if eng_name in (ACT, DVE, GP) and sem == ENG_SEM[eng_name]:
            v = self.cnt[sem]
            key = (eng_name, sem)
            if v > 0 and self.last_wait.get(key, -1) < v:
                self.last_wait[key] = v
                if eng_name == self.cur:
                    self.eng.wait_ge(self.sems[sem], v)
        if eng_name == self.cur:
            inst = fn(self.eng)
            inst.then_inc(self.sems[sem], inc)
        self.cnt[sem] += inc

    def mark(self, sem):
        return self.cnt[sem]

    def wait(self, eng_name, sem, val=None):
        v = self.cnt[sem] if val is None else val
        assert v <= self.cnt[sem], f"wait on future: {sem} {v} > {self.cnt[sem]}"
        if v <= 0:
            return
        if sem == ENG_SEM.get(eng_name):
            return
        key = (eng_name, sem)
        if self.last_wait.get(key, -1) >= v:
            return
        self.last_wait[key] = v
        if eng_name == self.cur:
            self.eng.wait_ge(self.sems[sem], v)


def build_nc():
    nc = bass.Bass("TRN2", target_bir_lowering=False)

    x_t = nc.dram_tensor("x_t", [C, N], F32, kind="ExternalInput")
    qkvw_t = nc.dram_tensor("qkvw_t", [C, 3 * C], F32, kind="ExternalInput")
    projw_t = nc.dram_tensor("projw_t", [C, C], F32, kind="ExternalInput")
    projb_t = nc.dram_tensor("projb_t", [128, CH], F32, kind="ExternalInput")
    gatew = nc.dram_tensor("gatew", [I3, C], F32, kind="ExternalInput")
    upw = nc.dram_tensor("upw", [I3, C], F32, kind="ExternalInput")
    downw = nc.dram_tensor("downw", [C, I3], F32, kind="ExternalInput")
    out_t = nc.dram_tensor("out_t", [C, N], F32, kind="ExternalOutput")

    r1_d = nc.dram_tensor("r1_d", [C, N], F32)
    t_d = nc.dram_tensor("t_d", [C, N], BF16)
    graw_d = nc.dram_tensor("graw_d", [I3, N], BF16)
    uraw_d = nc.dram_tensor("uraw_d", [I3, N], BF16)
    g2_d = nc.dram_tensor("g2_d", [I3, N], BF16)
    o_d = nc.dram_tensor("o_d", [I3, N], BF16)
    gdq_d = nc.dram_tensor("gdq_d", [I3, C], BF16)
    udq_d = nc.dram_tensor("udq_d", [I3, C], BF16)
    ddq_d = nc.dram_tensor("ddq_d", [C, I3], BF16)
    mx_d = nc.dram_tensor("mx_d", [1, 128], F32)
    ones_d = nc.dram_tensor("ones_d", [1, 64], BF16)
    cc_in = nc.dram_tensor("cc_in", [1, 2], F32)
    cc_out = nc.dram_tensor("cc_out", [1, 2], F32)

    ctx = ExitStack()
    with ctx:
        def sb(name, shape, dt=F32):
            return ctx.enter_context(nc.sbuf_tensor(name, shape, dt))

        def ps(name, shape):
            return ctx.enter_context(nc.psum_tensor(name, shape, F32))

        sems = {}
        for s_ in ("pe", "act", "dve", "gp", "cc",
                   "lsel", "lw1", "lw2", "lsc", "lx", "lt", "lu", "lo",
                   "lwT", "ld2", "sw0", "sw1", "sr1", "stt", "sf0", "sf1",
                   "sg2", "so", "sout", "lt2", "lu2", "sg3", "so2",
                   "lw3", "lw4", "sw2", "sw3"):
            sems[s_] = ctx.enter_context(nc.semaphore("s_" + s_))

        W1 = sb("W1", [128, 24576], BF16)
        W2a = sb("W2a", [128, 8192], BF16)
        W2b = sb("W2b", [128, 16384], BF16)
        BF1 = sb("BF1", [128, 4096], F32)
        OST = sb("OST", [128, 2048], F32)
        BIGI = sb("BIGI", [128, 1024], I32)
        BB1 = sb("BB1", [128, 4096], BF16)
        BB2 = sb("BB2", [128, 4096], BF16)
        BB3 = sb("BB3", [128, 4096], BF16)
        BB4 = sb("BB4", [128, 4096], BF16)
        ARB = sb("ARB", [128, 4096], BF16)
        fst = sb("fst", [128, 2048], BF16)

        stat = sb("stat", [128, 1024])
        Sf = sb("Sf", [128, 1024])
        Sb = sb("Sb", [128, 1024], BF16)
        zf = sb("zf", [128, 8])
        zbd = sb("zbd", [128, 16], BF16)
        d2 = sb("d2", [2, 4096], BF16)
        sel2 = sb("sel2", [2, 128], BF16)
        serb = sb("serb", [16, 512], BF16)
        rbcb = sb("rbcb", [128, 512], BF16)
        sel = sb("sel", [128, 128], BF16)
        selT = sb("selT", [16, 1024], BF16)
        ones = sb("ones", [128, 128], BF16)
        onecol = sb("onecol", [128, 1], BF16)
        mxacc = sb("mxacc", [128, 8])
        rowred = sb("rowred", [1, 128])
        scl = sb("scl", [128, 32])
        bm = sb("bm", [128, 32])
        bi = sb("bi", [128, 32])
        projb = sb("projb", [128, CH])

        p_mm = ps("p_mm", [128, 512])
        p_sum = ps("p_sum", [128, 512])
        p_sq = ps("p_sq", [128, 512])
        p_bc = ps("p_bc", [128, 512])
        p_att = ps("p_att", [128, 1024])
        p_S = ps("p_S", [128, 1024])

        block = ctx.enter_context(nc.Block())
        S = Sched(sems)

        sT, iT = 1, 2
        sG, rG, nbG, sG65, nbG65 = 5, 6, 7, 8, 9
        sU, rU = 10, 11
        s2, r2, s2su, nb2su = 13, 14, 15, 16
        sO, rO = 18, 19

        def uT_sl(k, oc):
            f = 3072 * k + 128 * oc
            return W2a[:, f:f + 128] if f < 8192 else W2b[:, f - 8192:f - 8192 + 128]

        def oq_sl(k):
            f = 512 * k
            return W2a[:, f:f + 512] if f + 512 <= 8192 else W2b[:, f - 8192:f - 8192 + 512]

        def program():
            S.do(DVE, lambda e: e.memset(ones[:], 1.0))
            S.do(DVE, lambda e: e.memset(onecol[:], 1.0))
            S.do(DVE, lambda e: e.memset(sel[:], 0.0))
            S.do(DVE, lambda e: e.memset(selT[:], 0.0))
            for c in range(CH):
                for j in range(2):
                    h = 2 * c + j
                    S.do(DVE, lambda e, c=c, j=j, h=h: e.memset(
                        sel[64 * j:64 * j + 64, 16 * c + h:16 * c + h + 1], 1.0))
            S.do(DVE, lambda e: e.memset(sel2[:], 0.0))
            S.do(DVE, lambda e: e.memset(Sf[:], 0.0))
            S.do(DVE, lambda e: e.memset(Sb[:], 0.0))
            S.do(DVE, lambda e: e.memset(zf[:], 0.0))
            S.do(DVE, lambda e: e.memset(zbd[:], 0.0))
            S.do(DVE, lambda e: e.memset(mxacc[:], 0.0))
            for col, v in ((24, LN_EPS), (25, -0.5), (26, 8.0), (27, -1.0),
                           (28, 32768.0), (29, -32768.0)):
                S.do(DVE, lambda e, col=col, v=v: e.memset(
                    scl[:, col:col + 1], v))
            consts_dve = S.mark("dve")

            S.wait(SY, "dve", consts_dve)
            S.do(SY, lambda e: e.dma_start(out=ones_d[:, :], in_=ones[0:1, 0:64]),
                 sem="lsel")
            S.wait(SY, "lsel")
            for c in range(CH):
                for j in range(2):
                    h = 2 * c + j
                    S.do(SY, lambda e, h=h, c=c, j=j: e.dma_start(
                        out=selT[h:h + 1, 128 * c + 64 * j:128 * c + 64 * j + 64],
                        in_=ones_d[:, :]), sem="lsel")
            for j in range(2):
                S.do(SY, lambda e, j=j: e.dma_start(
                    out=sel2[j:j + 1, 64 * j:64 * j + 64], in_=ones_d[:, :]),
                    sem="lsel")
            sel_ld = S.mark("lsel")

            for c in range(CH):
                S.do(SY, lambda e, c=c: e.dma_start(
                    out=BF1[:, 0:3072], in_=qkvw_t[128 * c:128 * (c + 1), :]),
                    sem="lw1")
                S.wait(ACT, "lw1")
                S.do(ACT, lambda e, c=c: e.activation(
                    W1[:, 3072 * c:3072 * (c + 1)], BF1[:, 0:3072], AF.Copy,
                    bias=0.0, scale=1.0))
                S.do(SY, lambda e, c=c: e.dma_start(
                    out=OST[:, 0:1024], in_=projw_t[128 * c:128 * (c + 1), :]),
                    sem="lw2")
                S.wait(ACT, "lw2")
                S.do(ACT, lambda e, c=c: e.activation(
                    W2a[:, 1024 * c:1024 * (c + 1)], OST[:, 0:1024], AF.Copy,
                    bias=0.0, scale=1.0))
                S.wait(SY, "act")
            S.do(SY, lambda e: e.dma_start(out=projb[:, :], in_=projb_t[:, :]),
                 sem="lsc")
            projb_ld = S.mark("lsc")

            S.wait(SY, "dve", consts_dve)
            tiles = []
            for k in range(OCH):
                tiles.append((gatew, gdq_d, 128 * k, 0))
            for k in range(OCH):
                tiles.append((upw, udq_d, 128 * k, 0))
            for rt in range(CH):
                for ct in range(3):
                    tiles.append((downw, ddq_d, 128 * rt, 1024 * ct))
            qslot_store_mark = {0: 0, 1: 0, 2: 0, 3: 0}
            LSEMS = ("lw1", "lw2", "lw3", "lw4")
            SSEMS = ("sw0", "sw1", "sw2", "sw3")
            for ti_, (src_, dst, r0, c0) in enumerate(tiles):
                hb = ti_ % 4
                w_sl = BF1[:, 1024 * hb:1024 * hb + 1024]
                bm_sl = (bm if hb % 2 == 0 else bi)[:, :]
                dq_sl = BB1[:, 1024 * hb:1024 * hb + 1024]
                lsem, ssem = LSEMS[hb], SSEMS[hb]
                S.wait(SY, ssem, qslot_store_mark[hb])
                S.do(SY, lambda e, src_=src_, r0=r0, c0=c0, w_sl=w_sl: e.dma_start(
                    out=w_sl, in_=src_[r0:r0 + 128, c0:c0 + 1024]), sem=lsem)
                S.wait(DVE, lsem)
                S.wait(DVE, ssem, qslot_store_mark[hb])
                S.do(DVE, lambda e, w_sl=w_sl, bm_sl=bm_sl: e.tensor_reduce(
                    bm_sl, w_sl.rearrange("p (b k) -> p b k", k=32),
                    axis=mybir.AxisListType.X, op=ALU.max,
                    apply_absolute_value=True))
                S.do(DVE, lambda e, bm_sl=bm_sl: e.tensor_scalar(
                    bm_sl, bm_sl, 1.0 / 7.0, 1e-12, op0=ALU.mult, op1=ALU.max))
                S.do(DVE, lambda e, bm_sl=bm_sl: e.reciprocal(
                    stat[:, 0:32], bm_sl))
                S.do(DVE, lambda e, bm_sl=bm_sl: e.tensor_tensor(
                    stat[:, 32:64], stat[:, 0:32], bm_sl, op=ALU.mult))
                S.do(DVE, lambda e: e.tensor_scalar(
                    stat[:, 32:64], stat[:, 32:64], -1.0, 2.0,
                    op0=ALU.mult, op1=ALU.add))
                S.do(DVE, lambda e: e.tensor_tensor(
                    stat[:, 0:32], stat[:, 0:32], stat[:, 32:64], op=ALU.mult))
                S.do(DVE, lambda e, w_sl=w_sl: e.tensor_tensor(
                    OST[:, 0:1024].rearrange("p (b k) -> p b k", k=32),
                    w_sl.rearrange("p (b k) -> p b k", k=32),
                    stat[:, 0:32].unsqueeze(2).broadcast_to((128, 32, 32)),
                    op=ALU.mult))
                S.do(DVE, lambda e: e.tensor_scalar(
                    OST[:, 1024:2048], OST[:, 0:1024], 8.0, None, op0=ALU.add))
                S.do(DVE, lambda e: e.tensor_copy(BIGI[:, :], OST[:, 1024:2048]))
                S.wait(ACT, "dve")
                S.do(ACT, lambda e: e.activation(
                    OST[:, 0:1024], BIGI[:, :], AF.Copy, bias=0.0, scale=1.0))
                S.wait(DVE, "act")
                S.do(DVE, lambda e, bm_sl=bm_sl, dq_sl=dq_sl: e.scalar_tensor_tensor(
                    dq_sl.rearrange("p (b k) -> p b k", k=32),
                    OST[:, 0:1024].rearrange("p (b k) -> p b k", k=32), -8.0,
                    bm_sl.unsqueeze(2).broadcast_to((128, 32, 32)),
                    op0=ALU.add, op1=ALU.mult))
                S.wait(GP, "dve")
                S.do(GP, lambda e, dst=dst, r0=r0, c0=c0, dq_sl=dq_sl: e.dma_start(
                    out=dst[r0:r0 + 128, c0:c0 + 1024], in_=dq_sl), sem=ssem)
                qslot_store_mark[hb] = S.mark(ssem)
            wdq_st0, wdq_st1 = S.mark("sw0"), S.mark("sw1")
            wdq_st2, wdq_st3 = S.mark("sw2"), S.mark("sw3")

            def ln_stats(src_bf_tile, sq_bf_tile, chunk_marks=None):
                for c in range(CH):
                    if chunk_marks is not None:
                        S.wait(PE, "act", chunk_marks[c])
                    S.do(PE, lambda e, c=c: e.matmul(
                        p_sum[:, :], ones[:], src_bf_tile(c),
                        start=(c == 0), stop=(c == CH - 1)))
                for c in range(CH):
                    S.do(PE, lambda e, c=c: e.matmul(
                        p_sq[:, :], ones[:], sq_bf_tile(c),
                        start=(c == 0), stop=(c == CH - 1)))
                S.wait(DVE, "pe")
                S.do(DVE, lambda e: e.tensor_scalar(
                    stat[:, 0:ST], p_sum[:, :], 1.0 / C, None, op0=ALU.mult))
                S.do(DVE, lambda e: e.tensor_tensor(
                    OST[:, 0:ST], stat[:, 0:ST], stat[:, 0:ST], op=ALU.mult))
                S.do(DVE, lambda e: e.scalar_tensor_tensor(
                    stat[:, ST:2 * ST], p_sq[:, :], 1.0 / C,
                    OST[:, 0:ST], op0=ALU.mult, op1=ALU.subtract))
                S.wait(ACT, "dve")
                S.do(ACT, lambda e: e.activation(
                    stat[:, ST:2 * ST], stat[:, ST:2 * ST], AF.Ln,
                    bias=scl[:, 24:25], scale=1.0))
                S.do(ACT, lambda e: e.activation(
                    stat[:, ST:2 * ST], stat[:, ST:2 * ST], AF.Exp,
                    bias=0.0, scale=scl[:, 25:26]))
                S.wait(DVE, "act")

            for s in range(NS):
                tok0 = ST * s
                if s == 0:
                    S.wait(SY, "sw0", wdq_st0)
                    S.wait(SY, "sw1", wdq_st1)
                    S.wait(SY, "sw2", wdq_st2)
                    S.wait(SY, "sw3", wdq_st3)
                S.wait(SY, "sr1")
                for c in range(CH):
                    S.do(SY, lambda e, c=c: e.dma_start(
                        out=BF1[:, ST * c:ST * (c + 1)],
                        in_=x_t[128 * c:128 * (c + 1), tok0:tok0 + ST]), sem="lx")
                x_ld = S.mark("lx")
                S.wait(ACT, "lx", x_ld)
                if s == 0:
                    S.wait(ACT, "dve", consts_dve)
                S.wait(ACT, "stt")
                if s == 0:
                    S.wait(ACT, "sw0", wdq_st0)
                    S.wait(ACT, "sw1", wdq_st1)
                    S.wait(ACT, "sw2", wdq_st2)
                    S.wait(ACT, "sw3", wdq_st3)
                ln1m = []
                for c in range(CH):
                    S.do(ACT, lambda e, c=c: e.activation(
                        BB1[:, ST * c:ST * (c + 1)], BF1[:, ST * c:ST * (c + 1)],
                        AF.Copy, bias=0.0, scale=1.0))
                    S.do(ACT, lambda e, c=c: e.activation(
                        BB2[:, ST * c:ST * (c + 1)], BF1[:, ST * c:ST * (c + 1)],
                        AF.Square, bias=0.0, scale=1.0))
                    ln1m.append(S.mark("act"))
                if s == 0:
                    S.wait(PE, "dve", consts_dve)
                ln_stats(lambda c: BB1[:, ST * c:ST * (c + 1)],
                         lambda c: BB2[:, ST * c:ST * (c + 1)], ln1m)
                for c in range(CH):
                    S.do(DVE, lambda e, c=c: e.tensor_tensor(
                        OST[:, 0:ST], BF1[:, ST * c:ST * (c + 1)],
                        stat[:, 0:ST], op=ALU.subtract))
                    S.do(DVE, lambda e, c=c: e.tensor_tensor(
                        BB1[:, ST * c:ST * (c + 1)], OST[:, 0:ST],
                        stat[:, ST:2 * ST], op=ALU.mult))
                ln1_dve = S.mark("dve")

                S.wait(PE, "dve", ln1_dve)
                if s == 0:
                    S.wait(PE, "lsel", sel_ld)
                qm = {}
                for oc in range(CH):
                    pp = p_mm if (oc % 2 == 0) else p_sq
                    if oc >= 2:
                        S.wait(PE, "act", qm[oc % 2])
                    for k in range(CH):
                        S.do(PE, lambda e, oc=oc, k=k, pp=pp: e.matmul(
                            pp[:, :], W1[:, 3072 * k + 128 * oc:3072 * k + 128 * (oc + 1)],
                            BB1[:, ST * k:ST * (k + 1)],
                            start=(k == 0), stop=(k == CH - 1)))
                    S.wait(ACT, "pe")
                    S.do(ACT, lambda e, oc=oc, pp=pp: e.activation(
                        BB2[:, ST * oc:ST * (oc + 1)], pp[:, :], AF.Exp,
                        bias=0.0, scale=1.0))
                    qm[oc % 2] = S.mark("act")
                qe_act = S.mark("act")
                S.wait(PE, "act", qe_act)
                for c in range(CH):
                    S.do(PE, lambda e, c=c: e.matmul(
                        p_bc[0:16, :], sel[:, 16 * c:16 * (c + 1)],
                        BB2[:, ST * c:ST * (c + 1)],
                        start=(c == 0), stop=(c == CH - 1)))
                S.wait(ACT, "pe")
                S.do(ACT, lambda e: e.activation(
                    serb[:, :], p_bc[0:16, :], AF.Ln, bias=0.0,
                    scale=scl[0:16, 26:27]))
                S.do(ACT, lambda e: e.activation(
                    serb[:, :], serb[:, :], AF.Exp, bias=0.0,
                    scale=scl[0:16, 27:28]))
                bcm = {}
                for c in range(CH):
                    pb = p_bc if (c % 2 == 0) else p_mm
                    rb = rbcb[:, 0:256].rearrange("p x -> p x") if False else None
                    if c < 2:
                        S.wait(PE, "act")
                    else:
                        S.wait(PE, "act", bcm[c % 2])
                    S.do(PE, lambda e, c=c, pb=pb: e.matmul(
                        pb[:, :], selT[:, 128 * c:128 * (c + 1)], serb[:, :],
                        start=True, stop=True))
                    S.wait(ACT, "pe")
                    S.wait(ACT, "dve")
                    S.do(ACT, lambda e, pb=pb: e.activation(
                        rbcb[:, :], pb[:, :], AF.Copy, bias=0.0, scale=1.0))
                    bcm[c % 2] = S.mark("act")
                    S.wait(DVE, "act")
                    S.do(DVE, lambda e, c=c: e.tensor_tensor(
                        BB2[:, ST * c:ST * (c + 1)], BB2[:, ST * c:ST * (c + 1)],
                        rbcb[:, :], op=ALU.mult))
                qs_dve = S.mark("dve")

                S.wait(PE, "dve", qs_dve)
                kvm = {}
                for tb in range(4):
                    for nch in range(4):
                        pp = p_mm if (nch % 2 == 0) else p_sq
                        if 4 * tb + nch >= 2:
                            S.wait(PE, "act", kvm[nch % 2])
                        for k in range(CH):
                            S.do(PE, lambda e, tb=tb, nch=nch, k=k, pp=pp: e.matmul(
                                pp[:, :],
                                BB1[:, ST * k + 128 * tb:ST * k + 128 * (tb + 1)],
                                W1[:, 3072 * k + 1024 + 512 * nch:3072 * k + 1024 + 512 * (nch + 1)],
                                start=(k == 0), stop=(k == CH - 1)))
                        S.wait(ACT, "pe")
                        if nch < 2:
                            S.do(ACT, lambda e, tb=tb, nch=nch, pp=pp: e.activation(
                                BB3[:, 1024 * tb + 512 * nch:1024 * tb + 512 * (nch + 1)],
                                pp[:, :], AF.Exp, bias=0.0, scale=1.0))
                        else:
                            S.do(ACT, lambda e, tb=tb, nch=nch, pp=pp: e.activation(
                                BB4[:, 1024 * tb + 512 * (nch - 2):1024 * tb + 512 * (nch - 1)],
                                pp[:, :], AF.Copy, bias=0.0, scale=1.0))
                        kvm[nch % 2] = S.mark("act")
                kv_act = S.mark("act")

                S.wait(PE, "act", kv_act)
                for ub in range(8):
                    u = 8 * s + ub
                    tb, half = ub // 2, ub % 2
                    toff = 128 * tb + 64 * half
                    S.wait(PE, "dve")
                    for q_ in range(CH):
                        S.do(PE, lambda e, q_=q_, tb=tb, half=half: e.matmul(
                            p_S[:, 128 * q_:128 * (q_ + 1)],
                            BB3[64 * half:64 * half + 64, 1024 * tb + 128 * q_:1024 * tb + 128 * (q_ + 1)],
                            BB4[64 * half:64 * half + 64, 1024 * tb + 128 * q_:1024 * tb + 128 * (q_ + 1)],
                            start=True, stop=True))
                        S.do(PE, lambda e, q_=q_, tb=tb, half=half: e.matmul(
                            p_bc[:, q_:q_ + 1],
                            BB3[64 * half:64 * half + 64, 1024 * tb + 128 * q_:1024 * tb + 128 * (q_ + 1)],
                            onecol[64 * half:64 * half + 64, :],
                            start=True, stop=True))
                    S.wait(PE, "act")
                    if u > 0:
                        for q_ in range(CH):
                            S.do(PE, lambda e, q_=q_, toff=toff: e.matmul(
                                p_att[:, 64 * q_:64 * (q_ + 1)],
                                Sb[:, 128 * q_:128 * (q_ + 1)],
                                BB2[:, ST * q_ + toff:ST * q_ + toff + 64],
                                start=True, stop=True))
                            S.do(PE, lambda e, q_=q_, toff=toff: e.matmul(
                                p_att[0:2, 512 + 64 * q_:512 + 64 * (q_ + 1)],
                                zbd[:, 2 * q_:2 * q_ + 2],
                                BB2[:, ST * q_ + toff:ST * q_ + toff + 64],
                                start=True, stop=True))
                    blk_pe = S.mark("pe")
                    S.wait(DVE, "pe", blk_pe)
                    if u > 0:
                        S.do(DVE, lambda e, ub=ub: e.tensor_copy(
                            d2.rearrange("p (q t) -> p q t", q=CH)[:, :, 64 * ub:64 * ub + 64],
                            p_att[0:2, 512:1024].rearrange("p (q t) -> p q t", q=CH)))
                        S.do(DVE, lambda e, toff=toff: e.tensor_copy(
                            ARB[:].rearrange("p (c t) -> p c t", c=CH)[:, :, toff:toff + 64],
                            p_att[:, 0:512].rearrange("p (c t) -> p c t", c=CH)))
                    else:
                        S.do(DVE, lambda e: e.memset(
                            d2.rearrange("p (q t) -> p q t", q=CH)[:, :, 0:64], 0.0))
                        S.do(DVE, lambda e: e.memset(
                            ARB[:].rearrange("p (c t) -> p c t", c=CH)[:, :, 0:64], 0.0))
                    S.do(DVE, lambda e: e.tensor_tensor(
                        Sf[:], Sf[:], p_S[:], op=ALU.add))
                    S.do(DVE, lambda e: e.tensor_tensor(
                        zf[:], zf[:], p_bc[:, 0:8], op=ALU.add))
                    blk_dve = S.mark("dve")
                    S.wait(ACT, "dve", blk_dve)
                    for j in range(2):
                        S.do(ACT, lambda e, j=j: e.activation(
                            Sb.rearrange("p (q j x) -> p q j x", q=CH, j=2)[64 * j:64 * j + 64, :, j, :],
                            Sf.rearrange("p (q j x) -> p q j x", q=CH, j=2)[64 * j:64 * j + 64, :, j, :],
                            AF.Copy, bias=0.0, scale=1.0))
                        S.do(ACT, lambda e, j=j: e.activation(
                            zbd.rearrange("p (q j) -> p q j", q=CH)[64 * j:64 * j + 64, :, j],
                            zf[64 * j:64 * j + 64, :],
                            AF.Copy, bias=0.0, scale=1.0))

                S.wait(DVE, "act")
                S.do(DVE, lambda e: e.tensor_scalar(
                    d2[:, :], d2[:, :], ATT_EPS, None, op0=ALU.max))
                S.wait(ACT, "dve")
                S.do(ACT, lambda e: e.activation(
                    d2[:, :], d2[:, :], AF.Ln, bias=0.0, scale=1.0))
                S.do(ACT, lambda e: e.activation(
                    d2[:, :], d2[:, :], AF.Exp, bias=0.0, scale=scl[0:2, 27:28]))
                dbm = {}
                for c in range(CH):
                    pb = p_bc if (c % 2 == 0) else p_mm
                    if c < 2:
                        S.wait(PE, "act")
                    else:
                        S.wait(PE, "act", dbm[c % 2])
                    S.do(PE, lambda e, c=c, pb=pb: e.matmul(
                        pb[:, :], sel2[:, :], d2[:, 512 * c:512 * (c + 1)],
                        start=True, stop=True))
                    S.wait(ACT, "pe")
                    S.wait(ACT, "dve")
                    S.do(ACT, lambda e, pb=pb: e.activation(
                        rbcb[:, :], pb[:, :], AF.Copy, bias=0.0, scale=1.0))
                    dbm[c % 2] = S.mark("act")
                    S.wait(DVE, "act")
                    S.do(DVE, lambda e, c=c: e.tensor_tensor(
                        ARB[:, ST * c:ST * (c + 1)], ARB[:, ST * c:ST * (c + 1)],
                        rbcb[:, :], op=ALU.mult))
                ab_dve = S.mark("dve")

                S.wait(PE, "dve", ab_dve)
                pjm = {}
                for oc in range(CH):
                    pp = p_mm if (oc % 2 == 0) else p_sq
                    if oc >= 2:
                        S.wait(PE, "act", pjm[oc % 2])
                    for k in range(CH):
                        S.do(PE, lambda e, oc=oc, k=k, pp=pp: e.matmul(
                            pp[:, :], W2a[:, 1024 * k + 128 * oc:1024 * k + 128 * (oc + 1)],
                            ARB[:, ST * k:ST * (k + 1)],
                            start=(k == 0), stop=(k == CH - 1)))
                    S.wait(ACT, "pe")
                    if s == 0 and oc == 0:
                        S.wait(ACT, "lsc", projb_ld)
                    S.do(ACT, lambda e, oc=oc, pp=pp: e.activation(
                        BB2[:, ST * oc:ST * (oc + 1)], pp[:, :], AF.Identity,
                        bias=projb[:, oc:oc + 1], scale=1.0))
                    S.do(ACT, lambda e, oc=oc, pp=pp: e.activation(
                        BB1[:, ST * oc:ST * (oc + 1)], pp[:, :], AF.Square,
                        bias=projb[:, oc:oc + 1], scale=1.0))
                    pjm[oc % 2] = S.mark("act")
                S.wait(PE, "act")
                ln_stats(lambda c: BB2[:, ST * c:ST * (c + 1)],
                         lambda c: BB1[:, ST * c:ST * (c + 1)])
                for h2 in range(2):
                    for c in range(CH):
                        off = ST * c + 256 * h2
                        S.do(ACT, lambda e, c=c, off=off: e.activation(
                            OST[:, 256 * c:256 * (c + 1)], BB2[:, off:off + 256],
                            AF.Copy, bias=0.0, scale=1.0))
                    S.wait(DVE, "act")
                    for c in range(CH):
                        off = ST * c + 256 * h2
                        S.do(DVE, lambda e, c=c, h2=h2: e.tensor_tensor(
                            OST[:, 256 * c:256 * (c + 1)], OST[:, 256 * c:256 * (c + 1)],
                            stat[:, 256 * h2:256 * h2 + 256], op=ALU.subtract))
                        S.do(DVE, lambda e, c=c, h2=h2: e.tensor_tensor(
                            OST[:, 256 * c:256 * (c + 1)], OST[:, 256 * c:256 * (c + 1)],
                            stat[:, ST + 256 * h2:ST + 256 * h2 + 256], op=ALU.mult))
                        S.do(DVE, lambda e, c=c, off=off: e.tensor_tensor(
                            BF1[:, off:off + 256], BF1[:, off:off + 256],
                            OST[:, 256 * c:256 * (c + 1)], op=ALU.add))
                    S.wait(ACT, "dve")
                r1_dve = S.mark("dve")
                S.wait(GP, "dve", r1_dve)
                for c in range(CH):
                    S.do(GP, lambda e, c=c: e.dma_start(
                        out=r1_d[128 * c:128 * (c + 1), tok0:tok0 + ST],
                        in_=BF1[:, ST * c:ST * (c + 1)]), sem="sr1")

                S.wait(ACT, "dve", r1_dve)
                ln3m = []
                for c in range(CH):
                    S.do(ACT, lambda e, c=c: e.activation(
                        BB1[:, ST * c:ST * (c + 1)], BF1[:, ST * c:ST * (c + 1)],
                        AF.Copy, bias=0.0, scale=1.0))
                    S.do(ACT, lambda e, c=c: e.activation(
                        BB2[:, ST * c:ST * (c + 1)], BF1[:, ST * c:ST * (c + 1)],
                        AF.Square, bias=0.0, scale=1.0))
                    ln3m.append(S.mark("act"))
                ln_stats(lambda c: BB1[:, ST * c:ST * (c + 1)],
                         lambda c: BB2[:, ST * c:ST * (c + 1)], ln3m)
                for h2 in range(2):
                    for c in range(CH):
                        off = ST * c + 256 * h2
                        S.do(DVE, lambda e, c=c, off=off, h2=h2: e.tensor_tensor(
                            OST[:, 256 * c:256 * (c + 1)], BF1[:, off:off + 256],
                            stat[:, 256 * h2:256 * h2 + 256], op=ALU.subtract))
                        S.do(DVE, lambda e, c=c, off=off, h2=h2: e.tensor_tensor(
                            BB1[:, off:off + 256], OST[:, 256 * c:256 * (c + 1)],
                            stat[:, ST + 256 * h2:ST + 256 * h2 + 256], op=ALU.mult))
                ln3_dve = S.mark("dve")
                S.do(DVE, lambda e: e.tensor_reduce(
                    mxacc[:, 5:6], BB1[:, :], axis=mybir.AxisListType.X,
                    op=ALU.max, apply_absolute_value=True))
                S.do(DVE, lambda e: e.tensor_tensor(
                    mxacc[:, 0:1], mxacc[:, 0:1], mxacc[:, 5:6], op=ALU.max))
                S.wait(GP, "dve", S.mark("dve"))
                for c in range(CH):
                    S.do(GP, lambda e, c=c: e.dma_start(
                        out=t_d[128 * c:128 * (c + 1), tok0:tok0 + ST],
                        in_=BB1[:, ST * c:ST * (c + 1)]), sem="stt")
                S.wait(SY, "dve", ln3_dve)
            phA_pe = S.mark("pe")
            phA_sr1, phA_stt = S.mark("sr1"), S.mark("stt")

            def global_max(src_col, n_scal, dst_col):
                for q_ in range(2):
                    sc = src_col + min(q_, n_scal - 1)
                    S.wait(SY, "dve")
                    S.do(SY, lambda e, sc=sc: e.dma_start(
                        out=mx_d.ap().rearrange("o p -> p o"),
                        in_=mxacc[:, sc:sc + 1]),
                        sem="lsc")
                    S.wait(SY, "lsc")
                    S.do(SY, lambda e: e.dma_start(
                        out=rowred[:, :], in_=mx_d[:, :]), sem="lsc")
                    S.wait(DVE, "lsc")
                    S.do(DVE, lambda e: e.tensor_reduce(
                        rowred[0:1, 0:1], rowred[0:1, 0:128],
                        axis=mybir.AxisListType.X, op=ALU.max,
                        apply_absolute_value=False))
                    S.wait(SY, "dve")
                    S.do(SY, lambda e, q_=q_: e.dma_start(
                        out=cc_in[0:1, q_:q_ + 1], in_=rowred[0:1, 0:1]), sem="lsc")
                S.wait(GP, "lsc")
                S.do(GP, lambda e: e.collective_compute(
                    "AllReduce", ALU.max,
                    replica_groups=[list(range(N_CORES))],
                    ins=[cc_in.ap()], outs=[cc_out.ap()]), sem="cc")
                S.wait(SY, "cc")
                S.do(SY, lambda e, dst_col=dst_col, n_scal=n_scal: e.dma_start(
                    out=scl[:, dst_col:dst_col + n_scal],
                    in_=cc_out[0:1, 0:n_scal].broadcast_to((128, n_scal))),
                    sem="lsc")
                S.wait(DVE, "lsc")

            S.wait(SY, "sw0", wdq_st0)
            S.wait(SY, "sw1", wdq_st1)
            S.wait(SY, "sw2", wdq_st2)
            S.wait(SY, "sw3", wdq_st3)
            S.wait(SY, "pe", phA_pe)
            for c in range(CH):
                S.do(SY, lambda e, c=c: e.dma_start_transpose(
                    W1[:, 3072 * c:3072 * (c + 1)], gdq_d[:, 128 * c:128 * (c + 1)]),
                    sem="lwT")
            for c in range(CH):
                f = 3072 * c
                if f + 3072 <= 8192:
                    S.do(SY, lambda e, c=c, f=f: e.dma_start_transpose(
                        W2a[:, f:f + 3072], udq_d[:, 128 * c:128 * (c + 1)]),
                        sem="lwT")
                elif f >= 8192:
                    S.do(SY, lambda e, c=c, f=f: e.dma_start_transpose(
                        W2b[:, f - 8192:f - 8192 + 3072],
                        udq_d[:, 128 * c:128 * (c + 1)]), sem="lwT")
                else:
                    lo = 8192 - f
                    S.do(SY, lambda e, c=c, f=f, lo=lo: e.dma_start_transpose(
                        W2a[:, f:8192], udq_d[0:lo, 128 * c:128 * (c + 1)]),
                        sem="lwT")
                    S.do(SY, lambda e, c=c, lo=lo: e.dma_start_transpose(
                        W2b[:, 0:3072 - lo], udq_d[lo:3072, 128 * c:128 * (c + 1)]),
                        sem="lwT")
            ffnw_ld = S.mark("lwT")

            global_max(0, 1, 0)
            S.do(DVE, lambda e: e.tensor_scalar(
                scl[:, sT:sT + 1], scl[:, 0:1], 1.0 / QMAX, 1e-12,
                op0=ALU.mult, op1=ALU.max))
            S.do(DVE, lambda e: e.reciprocal(scl[:, iT:iT + 1], scl[:, sT:sT + 1]))
            S.do(DVE, lambda e: e.tensor_tensor(
                scl[:, 30:31], scl[:, iT:iT + 1], scl[:, sT:sT + 1], op=ALU.mult))
            S.do(DVE, lambda e: e.tensor_scalar(
                scl[:, 30:31], scl[:, 30:31], -1.0, 2.0, op0=ALU.mult, op1=ALU.add))
            S.do(DVE, lambda e: e.tensor_tensor(
                scl[:, iT:iT + 1], scl[:, iT:iT + 1], scl[:, 30:31], op=ALU.mult))
            ar1_dve = S.mark("dve")

            pair_store_mark = {0: 0, 1: 0}
            for s in range(NS):
                tok0 = ST * s
                S.wait(SY, "act")
                S.wait(SY, "stt", phA_stt)
                for c in range(CH):
                    S.do(SY, lambda e, c=c: e.dma_start(
                        out=BB1[:, ST * c:ST * (c + 1)],
                        in_=t_d[128 * c:128 * (c + 1), tok0:tok0 + ST]), sem="lt")
                t_ld = S.mark("lt")
                S.wait(ACT, "lt", t_ld)
                S.wait(ACT, "dve", ar1_dve)
                S.wait(DVE, "pe")
                for c in range(CH):
                    S.do(ACT, lambda e, c=c: e.activation(
                        OST[:, 0:ST], BB1[:, ST * c:ST * (c + 1)], AF.Identity,
                        bias=scl[:, 28:29], scale=scl[:, iT:iT + 1]))
                    S.wait(DVE, "act")
                    S.do(DVE, lambda e: e.tensor_copy(BIGI[:, 0:ST], OST[:, 0:ST]))
                    S.do(DVE, lambda e, c=c: e.tensor_scalar(
                        BB2[:, ST * c:ST * (c + 1)], BIGI[:, 0:ST], 32768.0, None,
                        op0=ALU.subtract))
                    S.wait(ACT, "dve")
                tq_dve = S.mark("dve")
                S.wait(PE, "dve", tq_dve)
                if s == 0:
                    S.wait(PE, "lwT", ffnw_ld)
                f1m = {}
                for which, dst, mxcol in ((0, graw_d, 1), (1, uraw_d, 2)):
                    for oc in range(OCH):
                        pp = p_mm if (oc % 2 == 0) else p_sq
                        slot = oc % 4
                        pair = slot // 2
                        jdx = 24 * which + oc
                        if jdx >= 2:
                            S.wait(PE, "act", f1m[jdx % 2])
                        for k in range(CH):
                            lhs = (W1[:, 3072 * k + 128 * oc:3072 * k + 128 * (oc + 1)]
                                   if which == 0 else uT_sl(k, oc))
                            S.do(PE, lambda e, k=k, pp=pp, lhs=lhs: e.matmul(
                                pp[:, :], lhs, BB2[:, ST * k:ST * (k + 1)],
                                start=(k == 0), stop=(k == CH - 1)))
                        S.wait(ACT, "pe")
                        S.wait(ACT, "sf0" if pair == 0 else "sf1",
                               pair_store_mark[pair])
                        S.wait(ACT, "dve")
                        S.do(ACT, lambda e, slot=slot, pp=pp: e.activation(
                            fst[:, 512 * slot:512 * slot + 512],
                            pp[:, :], AF.Copy, bias=0.0, scale=1.0))
                        S.wait(DVE, "act")
                        S.do(DVE, lambda e, slot=slot: e.tensor_reduce(
                            mxacc[:, 5:6], fst[:, 512 * slot:512 * slot + 512],
                            axis=mybir.AxisListType.X, op=ALU.max,
                            apply_absolute_value=True))
                        S.do(DVE, lambda e, mxcol=mxcol: e.tensor_tensor(
                            mxacc[:, mxcol:mxcol + 1], mxacc[:, mxcol:mxcol + 1],
                            mxacc[:, 5:6], op=ALU.max))
                        f1m[jdx % 2] = S.mark("act")
                        if oc % 2 == 1:
                            S.wait(GP, "act")
                            S.do(GP, lambda e, oc=oc, dst=dst, tok0=tok0, pair=pair: e.dma_start(
                                out=dst[128 * (oc - 1):128 * (oc + 1), tok0:tok0 + ST].rearrange(
                                    "(a p) t -> p a t", p=128),
                                in_=fst[:, 1024 * pair:1024 * (pair + 1)].rearrange(
                                    "p (a t) -> p a t", a=2)),
                                sem="sf0" if pair == 0 else "sf1")
                            pair_store_mark[pair] = S.mark(
                                "sf0" if pair == 0 else "sf1")
            f1_pe = S.mark("pe")
            f1_sf0, f1_sf1 = S.mark("sf0"), S.mark("sf1")

            global_max(1, 2, 3)
            S.do(DVE, lambda e: e.tensor_tensor(
                scl[:, sG:sG + 1], scl[:, 3:4], scl[:, sT:sT + 1], op=ALU.mult))
            S.do(DVE, lambda e: e.tensor_scalar(
                scl[:, sG:sG + 1], scl[:, sG:sG + 1], 1.0 / QMAX, 1e-12,
                op0=ALU.mult, op1=ALU.max))
            S.do(DVE, lambda e: e.reciprocal(scl[:, 20:21], scl[:, sG:sG + 1]))
            S.do(DVE, lambda e: e.tensor_tensor(
                scl[:, 30:31], scl[:, 20:21], scl[:, sG:sG + 1], op=ALU.mult))
            S.do(DVE, lambda e: e.tensor_scalar(
                scl[:, 30:31], scl[:, 30:31], -1.0, 2.0, op0=ALU.mult, op1=ALU.add))
            S.do(DVE, lambda e: e.tensor_tensor(
                scl[:, 20:21], scl[:, 20:21], scl[:, 30:31], op=ALU.mult))
            S.do(DVE, lambda e: e.tensor_tensor(
                scl[:, rG:rG + 1], scl[:, sT:sT + 1], scl[:, 20:21], op=ALU.mult))
            S.do(DVE, lambda e: e.tensor_scalar(
                scl[:, nbG:nbG + 1], scl[:, sG:sG + 1], -32768.0, None, op0=ALU.mult))
            S.do(DVE, lambda e: e.tensor_scalar(
                scl[:, sG65:sG65 + 1], scl[:, sG:sG + 1], 1.0 / 65536.0, None,
                op0=ALU.mult))
            S.do(DVE, lambda e: e.tensor_scalar(
                scl[:, nbG65:nbG65 + 1], scl[:, sG65:sG65 + 1], -32768.0, None,
                op0=ALU.mult))
            S.do(DVE, lambda e: e.tensor_tensor(
                scl[:, sU:sU + 1], scl[:, 4:5], scl[:, sT:sT + 1], op=ALU.mult))
            S.do(DVE, lambda e: e.tensor_scalar(
                scl[:, sU:sU + 1], scl[:, sU:sU + 1], 1.0 / QMAX, 1e-12,
                op0=ALU.mult, op1=ALU.max))
            S.do(DVE, lambda e: e.reciprocal(scl[:, 21:22], scl[:, sU:sU + 1]))
            S.do(DVE, lambda e: e.tensor_tensor(
                scl[:, 30:31], scl[:, 21:22], scl[:, sU:sU + 1], op=ALU.mult))
            S.do(DVE, lambda e: e.tensor_scalar(
                scl[:, 30:31], scl[:, 30:31], -1.0, 2.0, op0=ALU.mult, op1=ALU.add))
            S.do(DVE, lambda e: e.tensor_tensor(
                scl[:, 21:22], scl[:, 21:22], scl[:, 30:31], op=ALU.mult))
            S.do(DVE, lambda e: e.tensor_tensor(
                scl[:, rU:rU + 1], scl[:, sT:sT + 1], scl[:, 21:22], op=ALU.mult))
            ar2_dve = S.mark("dve")

            f2_act_mark = {0: 0, 1: 0}
            f2_ldm = {0: 0, 1: 0}
            f2_stm = {0: 0, 1: 0}
            for r in range(OCH + 1):
                if r < OCH:
                    gbuf = BB1 if (r % 2 == 0) else BB3
                    lsem = "lt" if (r % 2 == 0) else "lt2"
                    S.wait(SY, "act", f2_act_mark[r % 2])
                    if r == 0:
                        S.wait(SY, "sf0", f1_sf0)
                        S.wait(SY, "sf1", f1_sf1)
                    S.do(SY, lambda e, r=r, gbuf=gbuf: e.dma_start(
                        out=gbuf[:, 0:2048],
                        in_=graw_d[128 * r:128 * (r + 1), 0:2048]), sem=lsem)
                    S.do(SY, lambda e, r=r, gbuf=gbuf: e.dma_start(
                        out=gbuf[:, 2048:4096],
                        in_=graw_d[128 * r:128 * (r + 1), 2048:4096]), sem=lsem)
                    f2_ldm[r % 2] = S.mark(lsem)
                if r == 0:
                    continue
                r_ = r - 1
                gbuf = BB1 if (r_ % 2 == 0) else BB3
                obuf = ARB if (r_ % 2 == 0) else BB4
                ssem = "sg2" if (r_ % 2 == 0) else "sg3"
                S.wait(ACT, "lt" if (r_ % 2 == 0) else "lt2", f2_ldm[r_ % 2])
                S.wait(ACT, "dve", ar2_dve)
                S.wait(ACT, "sr1")
                for q_ in range(4):
                    sl = slice(1024 * q_, 1024 * (q_ + 1))
                    fa = BF1[:, 0:1024]
                    fb = BF1[:, 1024:2048]
                    fc = BF1[:, 2048:3072]
                    S.wait(ACT, "dve")
                    S.do(ACT, lambda e, sl=sl, fa=fa, gbuf=gbuf: e.activation(
                        fa, gbuf[:, sl], AF.Identity, bias=scl[:, 28:29],
                        scale=scl[:, rG:rG + 1]))
                    S.wait(DVE, "act")
                    S.do(DVE, lambda e, fa=fa: e.tensor_copy(BIGI[:, :], fa))
                    S.wait(ACT, "dve")
                    S.do(ACT, lambda e, fb=fb: e.activation(
                        fb, BIGI[:, :], AF.Sigmoid, bias=scl[:, nbG:nbG + 1],
                        scale=scl[:, sG:sG + 1]))
                    S.do(ACT, lambda e, fa=fa: e.activation(
                        fa, BIGI[:, :], AF.Identity, bias=scl[:, nbG65:nbG65 + 1],
                        scale=scl[:, sG65:sG65 + 1]))
                    S.wait(DVE, "act")
                    S.do(DVE, lambda e, fb=fb: e.tensor_scalar(
                        fb, fb, 65536.0, 65535.49, op0=ALU.mult, op1=ALU.min))
                    S.do(DVE, lambda e, fb=fb: e.tensor_copy(BIGI[:, :], fb))
                    S.do(DVE, lambda e, fc=fc: e.tensor_copy(fc, BIGI[:, :]))
                    S.wait(DVE, ssem, f2_stm[r_ % 2])
                    S.do(DVE, lambda e, sl=sl, fa=fa, fc=fc, obuf=obuf: e.tensor_tensor(
                        obuf[:, sl], fa, fc, op=ALU.mult))
                    S.do(DVE, lambda e, sl=sl, obuf=obuf: e.tensor_reduce(
                        mxacc[:, 5:6], obuf[:, sl], axis=mybir.AxisListType.X,
                        op=ALU.max, apply_absolute_value=True))
                    S.do(DVE, lambda e: e.tensor_tensor(
                        mxacc[:, 3:4], mxacc[:, 3:4], mxacc[:, 5:6], op=ALU.max))
                f2_act_mark[r_ % 2] = S.mark("act")
                S.wait(GP, "dve")
                S.do(GP, lambda e, r_=r_, obuf=obuf: e.dma_start(
                    out=g2_d[128 * r_:128 * (r_ + 1), 0:2048], in_=obuf[:, 0:2048]),
                    sem=ssem)
                S.do(GP, lambda e, r_=r_, obuf=obuf: e.dma_start(
                    out=g2_d[128 * r_:128 * (r_ + 1), 2048:4096], in_=obuf[:, 2048:4096]),
                    sem=ssem)
                f2_stm[r_ % 2] = S.mark(ssem)
                S.wait(SY, "pe", f1_pe)
                S.do(SY, lambda e, r_=r_: e.dma_start_transpose(
                    W1[:, 1024 * r_:1024 * (r_ + 1)], ddq_d[:, 128 * r_:128 * (r_ + 1)]),
                    sem="ld2")
            dT_ld = S.mark("ld2")

            global_max(3, 1, 12)
            S.do(DVE, lambda e: e.tensor_scalar(
                scl[:, s2:s2 + 1], scl[:, 12:13], 1.0 / QMAX, 1e-12,
                op0=ALU.mult, op1=ALU.max))
            S.do(DVE, lambda e: e.reciprocal(scl[:, r2:r2 + 1], scl[:, s2:s2 + 1]))
            S.do(DVE, lambda e: e.tensor_tensor(
                scl[:, 30:31], scl[:, r2:r2 + 1], scl[:, s2:s2 + 1], op=ALU.mult))
            S.do(DVE, lambda e: e.tensor_scalar(
                scl[:, 30:31], scl[:, 30:31], -1.0, 2.0, op0=ALU.mult, op1=ALU.add))
            S.do(DVE, lambda e: e.tensor_tensor(
                scl[:, r2:r2 + 1], scl[:, r2:r2 + 1], scl[:, 30:31], op=ALU.mult))
            S.do(DVE, lambda e: e.tensor_tensor(
                scl[:, s2su:s2su + 1], scl[:, s2:s2 + 1], scl[:, sU:sU + 1],
                op=ALU.mult))
            S.do(DVE, lambda e: e.tensor_scalar(
                scl[:, nb2su:nb2su + 1], scl[:, s2su:s2su + 1], -32768.0, None,
                op0=ALU.mult))
            ar3_dve = S.mark("dve")

            f3_act_mark = {0: 0, 1: 0}
            f3_ldm = {0: 0, 1: 0}
            f3_stm = {0: 0, 1: 0}
            for r in range(OCH + 1):
                if r < OCH:
                    gbuf = BB1 if (r % 2 == 0) else BB3
                    ubuf = BB2 if (r % 2 == 0) else BB4
                    lsg = "lt" if (r % 2 == 0) else "lt2"
                    lsu = "lu" if (r % 2 == 0) else "lu2"
                    S.wait(SY, "act", f3_act_mark[r % 2])
                    if r == 0:
                        S.wait(SY, "sg2")
                        S.wait(SY, "sg3")
                    S.do(SY, lambda e, r=r, gbuf=gbuf: e.dma_start(
                        out=gbuf[:, 0:2048],
                        in_=g2_d[128 * r:128 * (r + 1), 0:2048]), sem=lsg)
                    S.do(SY, lambda e, r=r, gbuf=gbuf: e.dma_start(
                        out=gbuf[:, 2048:4096],
                        in_=g2_d[128 * r:128 * (r + 1), 2048:4096]), sem=lsg)
                    S.do(SY, lambda e, r=r, ubuf=ubuf: e.dma_start(
                        out=ubuf[:, 0:2048],
                        in_=uraw_d[128 * r:128 * (r + 1), 0:2048]), sem=lsu)
                    S.do(SY, lambda e, r=r, ubuf=ubuf: e.dma_start(
                        out=ubuf[:, 2048:4096],
                        in_=uraw_d[128 * r:128 * (r + 1), 2048:4096]), sem=lsu)
                    f3_ldm[r % 2] = (S.mark(lsg), S.mark(lsu))
                if r == 0:
                    continue
                r_ = r - 1
                gbuf = BB1 if (r_ % 2 == 0) else BB3
                ubuf = BB2 if (r_ % 2 == 0) else BB4
                obuf = ARB
                ssem = "so" if (r_ % 2 == 0) else "so2"
                S.wait(ACT, "lt" if (r_ % 2 == 0) else "lt2", f3_ldm[r_ % 2][0])
                S.wait(ACT, "lu" if (r_ % 2 == 0) else "lu2", f3_ldm[r_ % 2][1])
                S.wait(ACT, "dve", ar3_dve)
                for q_ in range(4):
                    sl = slice(1024 * q_, 1024 * (q_ + 1))
                    fa = BF1[:, 0:1024]
                    fb = BF1[:, 1024:2048]
                    S.wait(ACT, "dve")
                    S.do(ACT, lambda e, sl=sl, fa=fa, gbuf=gbuf: e.activation(
                        fa, gbuf[:, sl], AF.Identity, bias=scl[:, 28:29],
                        scale=scl[:, r2:r2 + 1]))
                    S.wait(DVE, "act")
                    S.do(DVE, lambda e, fa=fa: e.tensor_copy(BIGI[:, :], fa))
                    S.wait(ACT, "dve")
                    S.do(ACT, lambda e, fa=fa: e.activation(
                        fa, BIGI[:, :], AF.Identity, bias=scl[:, 29:30], scale=1.0))
                    S.do(ACT, lambda e, sl=sl, fb=fb, ubuf=ubuf: e.activation(
                        fb, ubuf[:, sl], AF.Identity, bias=scl[:, 28:29],
                        scale=scl[:, rU:rU + 1]))
                    S.wait(DVE, "act")
                    S.do(DVE, lambda e, fb=fb: e.tensor_copy(BIGI[:, :], fb))
                    S.wait(ACT, "dve")
                    S.do(ACT, lambda e, fb=fb: e.activation(
                        fb, BIGI[:, :], AF.Identity, bias=scl[:, nb2su:nb2su + 1],
                        scale=scl[:, s2su:s2su + 1]))
                    S.wait(DVE, "act")
                    S.wait(DVE, ssem, f3_stm[r_ % 2])
                    S.do(DVE, lambda e, sl=sl, fa=fa, fb=fb, obuf=obuf: e.tensor_tensor(
                        obuf[:, sl], fa, fb, op=ALU.mult))
                    S.do(DVE, lambda e, sl=sl, obuf=obuf: e.tensor_reduce(
                        mxacc[:, 5:6], obuf[:, sl], axis=mybir.AxisListType.X,
                        op=ALU.max, apply_absolute_value=True))
                    S.do(DVE, lambda e: e.tensor_tensor(
                        mxacc[:, 4:5], mxacc[:, 4:5], mxacc[:, 5:6], op=ALU.max))
                f3_act_mark[r_ % 2] = S.mark("act")
                S.wait(GP, "dve")
                S.do(GP, lambda e, r_=r_, obuf=obuf: e.dma_start(
                    out=o_d[128 * r_:128 * (r_ + 1), 0:2048], in_=obuf[:, 0:2048]),
                    sem=ssem)
                S.do(GP, lambda e, r_=r_, obuf=obuf: e.dma_start(
                    out=o_d[128 * r_:128 * (r_ + 1), 2048:4096], in_=obuf[:, 2048:4096]),
                    sem=ssem)
                f3_stm[r_ % 2] = S.mark(ssem)

            global_max(4, 1, 17)
            S.do(DVE, lambda e: e.tensor_scalar(
                scl[:, sO:sO + 1], scl[:, 17:18], 1.0 / QMAX, 1e-12,
                op0=ALU.mult, op1=ALU.max))
            S.do(DVE, lambda e: e.reciprocal(scl[:, rO:rO + 1], scl[:, sO:sO + 1]))
            S.do(DVE, lambda e: e.tensor_tensor(
                scl[:, 30:31], scl[:, rO:rO + 1], scl[:, sO:sO + 1], op=ALU.mult))
            S.do(DVE, lambda e: e.tensor_scalar(
                scl[:, 30:31], scl[:, 30:31], -1.0, 2.0, op0=ALU.mult, op1=ALU.add))
            S.do(DVE, lambda e: e.tensor_tensor(
                scl[:, rO:rO + 1], scl[:, rO:rO + 1], scl[:, 30:31], op=ALU.mult))
            ar4_dve = S.mark("dve")

            S.wait(PE, "ld2", dT_ld)
            S.wait(SY, "so")
            S.wait(SY, "so2")
            S.wait(SY, "sr1", phA_sr1)
            def o_slot(k):
                buf = (BB3, BB4, ARB)[k // 8]
                j = k % 8
                return buf[:, 512 * j:512 * (j + 1)]

            for s in range(NS):
                tok0 = ST * s
                S.wait(DVE, "pe")
                S.wait(SY, "act")
                for k in range(OCH):
                    S.do(SY, lambda e, k=k: e.dma_start(
                        out=o_slot(k),
                        in_=o_d[128 * k:128 * (k + 1), tok0:tok0 + ST]), sem="lo")
                S.wait(ACT, "lo", S.mark("lo"))
                S.wait(ACT, "dve", ar4_dve)
                S.wait(ACT, "dve")
                omark = {}
                for k in range(OCH):
                    if k >= 4:
                        S.wait(ACT, "dve", omark[k % 4])
                    S.do(ACT, lambda e, k=k: e.activation(
                        OST[:, 512 * (k % 4):512 * (k % 4) + 512], o_slot(k),
                        AF.Identity,
                        bias=scl[:, 28:29], scale=scl[:, rO:rO + 1]))
                    S.wait(DVE, "act")
                    S.do(DVE, lambda e, k=k: e.tensor_copy(
                        BIGI[:, 512 * (k % 2):512 * (k % 2) + 512],
                        OST[:, 512 * (k % 4):512 * (k % 4) + 512]))
                    omark[k % 4] = S.mark("dve")
                    S.do(DVE, lambda e, k=k: e.tensor_scalar(
                        oq_sl(k), BIGI[:, 512 * (k % 2):512 * (k % 2) + 512],
                        32768.0, None, op0=ALU.subtract))
                oq_dve = S.mark("dve")
                S.wait(SY, "sout")
                for c in range(CH):
                    S.do(SY, lambda e, c=c: e.dma_start(
                        out=BF1[:, ST * c:ST * (c + 1)],
                        in_=r1_d[128 * c:128 * (c + 1), tok0:tok0 + ST]), sem="lx")
                r1_ld = S.mark("lx")
                S.wait(PE, "dve", oq_dve)
                for oc in range(CH):
                    pp = p_mm if (oc % 2 == 0) else p_sq
                    for k in range(OCH):
                        S.do(PE, lambda e, oc=oc, k=k, pp=pp: e.matmul(
                            pp[:, :], W1[:, 1024 * k + 128 * oc:1024 * k + 128 * (oc + 1)],
                            oq_sl(k), start=(k == 0), stop=(k == OCH - 1)))
                    S.wait(ACT, "pe")
                    S.do(ACT, lambda e, oc=oc, pp=pp: e.activation(
                        BB1[:, ST * oc:ST * (oc + 1)], pp[:, :], AF.Copy,
                        bias=0.0, scale=scl[:, sO:sO + 1]))
                    S.do(ACT, lambda e, oc=oc, pp=pp: e.activation(
                        BB2[:, ST * oc:ST * (oc + 1)], pp[:, :], AF.Square,
                        bias=0.0, scale=scl[:, sO:sO + 1]))
                    S.wait(PE, "act")
                ln_stats(lambda c: BB1[:, ST * c:ST * (c + 1)],
                         lambda c: BB2[:, ST * c:ST * (c + 1)])
                S.wait(DVE, "lx", r1_ld)
                for h2 in range(2):
                    for c in range(CH):
                        off = ST * c + 256 * h2
                        S.do(ACT, lambda e, c=c, off=off: e.activation(
                            OST[:, 256 * c:256 * (c + 1)], BB1[:, off:off + 256],
                            AF.Copy, bias=0.0, scale=1.0))
                    S.wait(DVE, "act")
                    for c in range(CH):
                        off = ST * c + 256 * h2
                        S.do(DVE, lambda e, c=c, h2=h2: e.tensor_tensor(
                            OST[:, 256 * c:256 * (c + 1)], OST[:, 256 * c:256 * (c + 1)],
                            stat[:, 256 * h2:256 * h2 + 256], op=ALU.subtract))
                        S.do(DVE, lambda e, c=c, h2=h2: e.tensor_tensor(
                            OST[:, 256 * c:256 * (c + 1)], OST[:, 256 * c:256 * (c + 1)],
                            stat[:, ST + 256 * h2:ST + 256 * h2 + 256], op=ALU.mult))
                        S.do(DVE, lambda e, c=c, off=off: e.tensor_tensor(
                            BF1[:, off:off + 256], BF1[:, off:off + 256],
                            OST[:, 256 * c:256 * (c + 1)], op=ALU.add))
                    S.wait(ACT, "dve")
                out_dve = S.mark("dve")
                S.wait(GP, "dve", out_dve)
                for c in range(CH):
                    S.do(GP, lambda e, c=c: e.dma_start(
                        out=out_t[128 * c:128 * (c + 1), tok0:tok0 + ST],
                        in_=BF1[:, ST * c:ST * (c + 1)]), sem="sout")

            S.wait(GP, "sout", S.mark("sout"))
            S.wait(SY, "sout", S.mark("sout"))

        @block.sync
        def _(e):
            S.begin(SY, e)
            program()

        @block.tensor
        def _(e):
            S.begin(PE, e)
            program()

        @block.vector
        def _(e):
            S.begin(DVE, e)
            program()

        @block.scalar
        def _(e):
            S.begin(ACT, e)
            program()

        @block.gpsimd
        def _(e):
            S.begin(GP, e)
            program()

    return nc


def kernel(L0_attn_x, L0_attn_qkv_w, L0_attn_proj_w, L0_attn_proj_b,
           L0_ffn_w_gate, L0_ffn_w_up, L0_ffn_w_down):
    x = np.asarray(L0_attn_x, dtype=np.float32)
    qkvw_t = np.ascontiguousarray(np.asarray(L0_attn_qkv_w, np.float32).T)
    projw_t = np.ascontiguousarray(np.asarray(L0_attn_proj_w, np.float32).T)
    projb_t = np.ascontiguousarray(
        np.asarray(L0_attn_proj_b, np.float32).reshape(CH, 128).T)
    gatew = np.ascontiguousarray(np.asarray(L0_ffn_w_gate, np.float32))
    upw = np.ascontiguousarray(np.asarray(L0_ffn_w_up, np.float32))
    downw = np.ascontiguousarray(np.asarray(L0_ffn_w_down, np.float32))

    nc = build_nc()
    in_maps = []
    for b in range(N_CORES):
        in_maps.append(dict(
            x_t=np.ascontiguousarray(x[b].T),
            qkvw_t=qkvw_t, projw_t=projw_t, projb_t=projb_t,
            gatew=gatew, upw=upw, downw=downw,
        ))
    res = run_bass_kernel_spmd(nc, in_maps, core_ids=list(range(N_CORES)))
    out = np.stack([np.asarray(res.results[b]["out_t"], np.float32).T
                    for b in range(N_CORES)], axis=0)
    return np.ascontiguousarray(out)


if __name__ == "__main__":
    nc = build_nc()
    print("build ok")
